# version 07
# baseline (speedup 1.0000x reference)
import contextlib
import numpy as np
import ml_dtypes
import concourse.bass as bass
import concourse.mybir as mybir
from concourse.bass_utils import run_bass_kernel_spmd

F32 = mybir.dt.float32
BF16 = mybir.dt.bfloat16
I32 = mybir.dt.int32
AF = mybir.ActivationFunctionType
ALU = mybir.AluOpType
AX = mybir.AxisListType

NB, L, DM, HD, NST, RK, NEXP, II = 4, 4096, 2048, 1024, 16, 128, 17, 1024
TOK = 2048
SUB = 512
NSUB = TOK // SUB
LC = 256
NCHUNK = L // LC
NDT = HD // 128
LN_EPS = 1e-5
NTILE = TOK // 128
DEBUG = False

ENGS = ("tensor", "vector", "scalar", "gpsimd", "sync")
EORDER = [16] + list(range(16))


class Sch:
    def __init__(self):
        self.steps = {e: [] for e in ENGS}
        self.ct = {}
        self.last_wait = {e: {} for e in ENGS}

    def op(self, eng, fn, inc=None, n=1):
        self.steps[eng].append(("o", fn, inc, n))
        if inc is not None:
            self.ct[inc] = self.ct.get(inc, 0) + n
            return (inc, self.ct[inc], eng)
        return None

    def dma(self, eng, fn, sem):
        return self.op(eng, fn, inc=sem, n=16)

    def mark(self, sem):
        return (sem, self.ct.get(sem, 0), None)

    def wait(self, eng, mark, force=False):
        if mark is None:
            return
        sem, val, peng = mark
        if val <= 0:
            return
        if peng == eng and not force:
            return
        lw = self.last_wait[eng]
        if lw.get(sem, -1) >= val:
            return
        lw[sem] = val
        self.steps[eng].append(("w", sem, val))


def build_program():
    nc = bass.Bass("TRN2", target_bir_lowering=False)
    ap = {}

    def din(name, shape, dt):
        ap[name] = nc.dram_tensor(name, shape, dt, kind="ExternalInput").ap()

    def dout(name, shape, dt):
        ap[name] = nc.dram_tensor(name, shape, dt, kind="ExternalOutput").ap()

    def dint(name, shape, dt, dbg=False):
        kind = "ExternalOutput" if (DEBUG and dbg) else "Internal"
        ap[name] = nc.dram_tensor(name, shape, dt, kind=kind).ap()

    din("xscan", [L, DM], F32)
    din("xsdh", [L, HD], F32)
    din("xmine", [TOK, DM], F32)
    din("xpT", [DM, 160], F32)
    din("wdtT", [RK, HD], F32)
    din("dtb", [128, NDT], F32)
    din("alog", [128, NDT * NST], F32)
    din("dvec", [128, NDT], F32)
    din("gw", [DM, 16], F32)
    din("ebias", [128, 16], F32)
    din("wg", [NEXP, DM, II], F32)
    din("wu", [NEXP, DM, II], F32)
    din("wd", [NEXP, II, DM], F32)
    din("identf", [128, 128], F32)
    din("sel", [1, 8], I32)
    dout("out", [TOK, DM], F32)

    dint("w16", [NEXP, 3, DM * II], BF16)
    dint("dtl_dram", [RK, L], BF16, dbg=True)
    dint("bc_dram", [32, L], F32, dbg=True)
    dint("xnT_dram", [128, NSUB * 16 * SUB], BF16, dbg=True)
    dint("sT_all", [L, HD], F32, dbg=True)
    dint("ag_in", [TOK, HD], F32)
    dint("ag_out", [2 * TOK, HD], F32)
    dint("agb_in", [2, 64], F32)
    dint("agb_out", [4, 64], F32)
    dint("smine_dram", [TOK, HD], F32, dbg=True)
    dint("sother_dram", [TOK, HD], F32, dbg=True)
    dint("routed_dram", [128, NTILE * DM], F32, dbg=True)

    sch = Sch()
    es = contextlib.ExitStack()
    es_scan = contextlib.ExitStack()
    es_p0 = contextlib.ExitStack()
    es_moe = contextlib.ExitStack()

    with es:
        sb = {}

        def t(name, shape, dt, stack=None):
            sb[name] = (stack or es).enter_context(nc.sbuf_tensor(name, shape, dt))
            return sb[name]

        t("xnT", [128, 16 * SUB], BF16)
        t("routed", [128, (SUB // 128) * DM], F32)
        t("comb", [128, NTILE * 16], F32)
        t("statsA", [128, 2 * (L // 128)], F32)
        t("A_sb", [128, NDT * NST], F32)
        t("wdtT_sb", [128, HD], BF16)
        t("ebias_sb", [128, 16], F32)
        t("dvec_sb", [128, NDT], F32)
        t("dtb_sb", [128, NDT], F32)
        t("identf_sb", [128, 128], F32)
        t("ones_sb", [1, 128], F32)
        t("sel_sb", [1, 8], I32)
        t("state", [128, NST * NDT], F32)
        t("stats1", [128, 12], F32)
        t("epscol", [128, 1], F32)
        t("negcol", [128, 1], F32)

        t("xr", [128, 2 * 512], F32, es_scan)
        t("xnsc", [128, 4 * LC], F32, es_scan)
        t("dtt", [128, 8 * LC], F32, es_scan)
        t("dtx", [128, 8 * LC], F32, es_scan)
        t("ybuf", [128, 8 * LC], F32, es_scan)
        t("hbuf", [128, 4 * LC], F32, es_scan)
        t("ubuf", [128, 4 * LC], F32, es_scan)
        t("scr2", [128, 2 * LC], F32, es_scan)
        t("dtl", [128, 2 * LC], BF16, es_scan)
        t("bcrow", [1, 3 * 2048], F32, es_scan)
        t("sTst", [128, LC], F32, es_scan)

        t("xpT_sb", [128, 16 * 160], BF16, es_p0)
        t("gw_sb", [128, 16 * 16], F32, es_p0)
        t("p0x", [128, 2 * DM], F32, es_p0)
        t("p0sq", [128, DM], F32, es_p0)
        t("p0xn", [128, 4 * DM], F32, es_p0)
        t("p0bt", [128, 512], BF16, es_p0)
        t("p0ft", [128, 512], F32, es_p0)
        t("p0dtl", [128, 2 * 512], BF16, es_p0)
        t("p0bc", [32, 2 * 512], F32, es_p0)
        t("p0sc", [128, 16], F32, es_p0)
        t("p0key", [128, 16], F32, es_p0)
        t("p0eq", [128, 16], F32, es_p0)
        t("p0acc", [128, 16], F32, es_p0)
        t("p0tmp", [128, 16], F32, es_p0)
        t("p0w", [128, 4], F32, es_p0)

        ps = {}
        for i in range(8):
            ps[i] = es.enter_context(nc.psum_tensor(f"pb{i}", [128, 512], F32))

        sems = {}
        for s_ in ("dSm", "dXa", "dXb", "dxr", "dtl0", "dtl1", "dbc0", "dbc1", "dbc2",
                   "dfin", "dout", "ddbg", "dSp0", "dSp1", "dxt", "dsT",
                   "dwc0", "dwc1", "dwc2", "dwc3", "dwd", "dxn", "drs",
                   "dW", "dG", "dag", "cc", "sP", "sV", "sA"):
            sems[s_] = es.enter_context(nc.semaphore(s_))

        def mm(lhsT, rhs, out, start, stop, inc=True):
            return sch.op("tensor",
                          lambda e, a=out, b=lhsT, c=rhs, s1=start, s2=stop:
                          e.matmul(a, b, c, start=s1, stop=s2),
                          inc="sP" if inc else None)

        def tp(in_ap, out_ap, inc=True):
            return sch.op("tensor",
                          lambda e, a=out_ap, b=in_ap: e.transpose(a, b, sb["identf_sb"][:]),
                          inc="sP" if inc else None)

        def sload(dst_ap, src_ap, sem="dSm", queue="sync"):
            return sch.dma(queue, lambda e, d=dst_ap, s=src_ap: e.dma_start(d, s), sem)

        m_xpT = sload(sb["xpT_sb"].ap().rearrange("p (kt m) -> p kt m", kt=16),
                      ap["xpT"].rearrange("(kt p) m -> p kt m", p=128),
                      sem="dG", queue="gpsimd")
        m_wdt = sload(sb["wdtT_sb"][:], ap["wdtT"], sem="dG", queue="gpsimd")
        m_gw = sload(sb["gw_sb"].ap().rearrange("p (dt e) -> p dt e", dt=16),
                     ap["gw"].rearrange("(dt p) e -> p dt e", p=128))
        m_eb = sload(sb["ebias_sb"][:], ap["ebias"])
        m_dv = sload(sb["dvec_sb"][:], ap["dvec"])
        m_db = sload(sb["dtb_sb"][:], ap["dtb"])
        m_alog = sload(sb["A_sb"][:], ap["alog"])
        m_idf = sload(sb["identf_sb"][:], ap["identf"])
        m_sel = sload(sb["sel_sb"][:], ap["sel"])

        m_total_sm = sch.mark("dSm")
        m_xpT = m_wdt = sch.mark("dG")
        m_gw = m_eb = m_dv = m_db = m_alog = m_idf = m_sel = m_total_sm
        m_ones = sch.op("vector", lambda e: e.memset(sb["ones_sb"][:], 1.0), inc="sV")
        m_eps = sch.op("vector", lambda e: e.memset(sb["epscol"][:], LN_EPS), inc="sV")
        m_neg = sch.op("vector", lambda e: e.memset(sb["negcol"][:], -1e30), inc="sV")
        m_state0 = sch.op("vector", lambda e: e.memset(sb["state"][:], 0.0), inc="sV")

        sch.wait("scalar", m_alog)
        m_aexp = sch.op("scalar",
                        lambda e: e.activation(sb["A_sb"][:], sb["A_sb"][:], AF.Exp),
                        inc="sA")
        sch.wait("vector", m_aexp)
        m_aneg = sch.op("vector",
                        lambda e: e.tensor_scalar_mul(sb["A_sb"][:], sb["A_sb"][:], -1.0),
                        inc="sV")

        wsrc = {0: ap["wg"], 1: ap["wu"], 2: ap["wd"]}
        wcast_mark = {}
        for e_i in EORDER:
            for m_i in range(3):
                csrc = wsrc[m_i][e_i].rearrange("a b -> (a b)").rearrange("(o f) -> o f", o=1)
                cdst = ap["w16"][e_i, m_i].rearrange("(o f) -> o f", o=1)
                wcast_mark[(e_i, m_i)] = sch.dma(
                    "gpsimd", lambda e, d=cdst, s=csrc: e.dma_start(d, s), "dW")

        state_h = {"prev_xn": None}

        def stats_and_xn(src_ap_t, row0, xn_out_ap, xt_slot, rstd, nmr, m_xt_free):
            c = sb["stats1"]
            sch.wait("sync", m_xt_free)
            xts = sb["p0x"][:, xt_slot * DM:(xt_slot + 1) * DM]
            mload = sch.dma("sync",
                            lambda e, d=xts, s=src_ap_t[row0:row0 + 128, :]:
                            e.dma_start(d, s), "dXa" if xt_slot == 0 else "dXb")
            sch.wait("vector", mload)
            sch.wait("vector", state_h["prev_xn"])

            def vop(fn):
                mk = sch.op("vector", fn, inc="sV")
                sch.wait("vector", mk, force=True)
                return mk

            vop(lambda e, x=xts: e.reduce_sum(c[:, 0:1], x, AX.X))
            vop(lambda e, x=xts: e.tensor_mul(sb["p0sq"][:], x, x))
            vop(lambda e: e.reduce_sum(c[:, 1:2], sb["p0sq"][:], AX.X))
            vop(lambda e: e.tensor_scalar_mul(c[:, 2:3], c[:, 0:1], 1.0 / DM))
            vop(lambda e: e.tensor_scalar_mul(c[:, 3:4], c[:, 1:2], 1.0 / DM))
            vop(lambda e: e.tensor_mul(c[:, 4:5], c[:, 2:3], c[:, 2:3]))
            m_var = vop(lambda e: e.tensor_sub(c[:, 5:6], c[:, 3:4], c[:, 4:5]))
            sch.wait("scalar", m_var)
            sch.wait("scalar", m_eps)
            m_ln = sch.op("scalar",
                          lambda e: e.activation(c[:, 6:7], c[:, 5:6], AF.Ln,
                                                 bias=sb["epscol"][:]), inc="sA")
            sch.wait("scalar", m_ln, force=True)
            m_r = sch.op("scalar",
                         lambda e, r=rstd: e.activation(r, c[:, 6:7], AF.Exp, scale=-0.5),
                         inc="sA")
            sch.wait("vector", m_r)
            m_nm = vop(lambda e, r=rstd, o=nmr:
                       e.scalar_tensor_tensor(o, c[:, 2:3], -1.0, r, ALU.mult, ALU.mult))
            sch.wait("scalar", m_nm)
            m_xn = sch.op("scalar",
                          lambda e, o=xn_out_ap, x=xts, r=rstd, n=nmr:
                          e.activation(o, x, AF.Identity, scale=r, bias=n),
                          inc="sA")
            state_h["prev_xn"] = m_xn
            return m_xn

        sch.wait("tensor", m_idf)
        sch.wait("tensor", m_xpT)
        sch.wait("tensor", m_gw)
        sch.wait("tensor", m_wdt)
        sch.wait("scalar", m_db)
        sch.wait("vector", m_dv)
        m_xt_free = [None, None]
        m_p0xn_free = [None] * 4
        m_bank7_free = None
        m_bank45_free = [None, None]
        m_p0stage_free = [None, None]
        m_p0bt_free = None

        for g in range(L // 512):
            m_xn_tiles = []
            for tt in range(4):
                ti = g * 4 + tt
                sch.wait("scalar", m_p0xn_free[tt])
                m_xn = stats_and_xn(ap["xscan"], g * 512 + tt * 128,
                                    sb["p0xn"][:, tt * DM:(tt + 1) * DM], tt % 2,
                                    sb["statsA"][:, 2 * ti:2 * ti + 1],
                                    sb["statsA"][:, 2 * ti + 1:2 * ti + 2],
                                    m_xt_free[tt % 2])
                m_xt_free[tt % 2] = sch.mark("sA")
                m_xn_tiles.append(m_xn)
            for dt_i in range(16):
                for tt in range(4):
                    sch.wait("tensor", m_xn_tiles[tt])
                sch.wait("tensor", m_bank7_free)
                for tt in range(4):
                    m_tp = tp(sb["p0xn"][:, tt * DM + dt_i * 128:tt * DM + (dt_i + 1) * 128],
                              ps[7][:, tt * 128:(tt + 1) * 128], inc=(tt == 3))
                sch.wait("vector", m_tp)
                sch.wait("vector", m_p0bt_free)
                m_ev = sch.op("vector",
                              lambda e: e.tensor_copy(sb["p0bt"][:], ps[7][:]),
                              inc="sV")
                m_bank7_free = m_ev
                sch.wait("tensor", m_ev)
                if dt_i == 0:
                    sch.wait("tensor", m_bank45_free[0])
                    sch.wait("tensor", m_bank45_free[1])
                mm(sb["xpT_sb"][:, dt_i * 160:dt_i * 160 + 128], sb["p0bt"][:],
                   ps[4][:, 0:512], start=(dt_i == 0), stop=(dt_i == 15))
                m_xdbl = mm(sb["xpT_sb"][:, dt_i * 160 + 128:dt_i * 160 + 160],
                            sb["p0bt"][:], ps[5][0:32, 0:512],
                            start=(dt_i == 0), stop=(dt_i == 15))
                m_p0bt_free = m_xdbl
            for tt in range(4):
                m_p0xn_free[tt] = sch.mark("sP")
            par = g % 2
            sch.wait("vector", m_xdbl)
            sch.wait("vector", m_p0stage_free[par])
            m_e1 = sch.op("vector",
                          lambda e, p=par: e.tensor_copy(
                              sb["p0dtl"][:, p * 512:(p + 1) * 512], ps[4][:]), inc="sV")
            m_e2 = sch.op("vector",
                          lambda e, p=par: e.tensor_copy(
                              sb["p0bc"][0:32, p * 512:(p + 1) * 512], ps[5][0:32, :]),
                          inc="sV")
            m_bank45_free = [m_e1, m_e2]
            sch.wait("scalar", m_e2)
            sch.dma("scalar",
                    lambda e, p=par, gg=g: e.dma_start(
                        ap["dtl_dram"][:, gg * 512:(gg + 1) * 512],
                        sb["p0dtl"][:, p * 512:(p + 1) * 512]), "dSp0" if par == 0 else "dSp1")
            m_d2 = sch.dma("scalar",
                           lambda e, p=par, gg=g: e.dma_start(
                               ap["bc_dram"][:, gg * 512:(gg + 1) * 512],
                               sb["p0bc"][0:32, p * 512:(p + 1) * 512]),
                           "dSp0" if par == 0 else "dSp1")
            m_p0stage_free[par] = m_d2
        m_p0a_dma = [sch.mark("dSp0"), sch.mark("dSp1")]
        m_p0a_v = sch.mark("sV")

        m_logit_free = [None] * 4
        m_stagef_free = None
        m_xnTst_free = m_p0bt_free
        m_route_free = None

        for g in range(NSUB):
            m_xn_tiles = []
            for tt in range(4):
                sch.wait("scalar", m_p0xn_free[tt])
                m_xn = stats_and_xn(ap["xmine"], g * 512 + tt * 128,
                                    sb["p0xn"][:, tt * DM:(tt + 1) * DM], tt % 2,
                                    sb["stats1"][:, 8:9], sb["stats1"][:, 9:10],
                                    m_xt_free[tt % 2])
                m_xt_free[tt % 2] = sch.mark("sA")
                m_xn_tiles.append(m_xn)
            for dt_i in range(16):
                for tt in range(4):
                    sch.wait("tensor", m_xn_tiles[tt])
                sch.wait("tensor", m_bank7_free)
                for tt in range(4):
                    m_tp = tp(sb["p0xn"][:, tt * DM + dt_i * 128:tt * DM + (dt_i + 1) * 128],
                              ps[7][:, tt * 128:(tt + 1) * 128], inc=(tt == 3))
                sch.wait("vector", m_tp)
                sch.wait("vector", m_xnTst_free)
                m_evb = sch.op("vector",
                               lambda e: e.tensor_copy(sb["p0bt"][:], ps[7][:]), inc="sV")
                sch.wait("vector", m_stagef_free)
                m_evf = sch.op("vector",
                               lambda e: e.tensor_copy(sb["p0ft"][:], ps[7][:]), inc="sV")
                m_bank7_free = m_evf
                sch.wait("scalar", m_evb)
                m_xnTst_free = sch.dma(
                    "scalar",
                    lambda e, gg=g, d=dt_i: e.dma_start(
                        ap["xnT_dram"][:, gg * 16 * 512 + d * 512:gg * 16 * 512 + (d + 1) * 512],
                        sb["p0bt"][:]), "dxt")
                sch.wait("tensor", m_evf)
                for tt in range(4):
                    if dt_i == 0:
                        sch.wait("tensor", m_logit_free[tt])
                    m_lg = mm(sb["p0ft"][:, tt * 128:(tt + 1) * 128],
                              sb["gw_sb"].ap().rearrange("p (dt e) -> p dt e", dt=16)[:, dt_i, :],
                              ps[tt][:, 0:16], start=(dt_i == 0), stop=(dt_i == 15))
                m_stagef_free = m_lg
            for tt in range(4):
                tile_abs = g * 4 + tt
                sch.wait("scalar", m_lg)
                sch.wait("scalar", m_route_free)
                sch.wait("vector", m_eb)
                m_sc0 = sch.op("scalar",
                               lambda e, b=tt: e.activation(sb["p0tmp"][:], ps[b][:, 0:16],
                                                            AF.Exp, scale=-1.0), inc="sA")
                sch.op("scalar",
                       lambda e: e.activation(sb["p0eq"][:], sb["p0tmp"][:], AF.Ln,
                                              bias=1.0))
                m_sc = sch.op("scalar",
                              lambda e: e.activation(sb["p0sc"][:], sb["p0eq"][:], AF.Exp,
                                                     scale=-1.0), inc="sA")
                m_logit_free[tt] = m_sc0
                if DEBUG and tile_abs == 0:
                    sch.wait("sync", m_sc)
                    sch.dma("sync", lambda e: e.dma_start(ap["dbg_sc0"], sb["p0sc"][:]), "ddbg")
                sch.wait("vector", m_sc)
                m_ka = sch.op("vector",
                              lambda e: e.tensor_add(sb["p0key"][:], sb["p0sc"][:],
                                                     sb["ebias_sb"][:]), inc="sV")
                sch.op("vector", lambda e: e.memset(sb["p0acc"][:], 0.0))
                c = sb["p0w"]
                for j in range(4):
                    sch.wait("vector", m_ka, force=True)
                    m_mj = sch.op("vector",
                                  lambda e: e.reduce_max(c[:, 0:1], sb["p0key"][:], AX.X),
                                  inc="sV")
                    sch.wait("vector", m_mj, force=True)
                    sch.op("vector",
                           lambda e: e.tensor_scalar(sb["p0eq"][:], sb["p0key"][:],
                                                     c[:, 0:1], None, ALU.is_equal))
                    sch.wait("vector", m_neg)
                    m_ka = sch.op("vector",
                                  lambda e: e.scalar_tensor_tensor(
                                      sb["p0key"][:], sb["p0eq"][:], sb["negcol"][:],
                                      sb["p0key"][:], ALU.mult, ALU.add), inc="sV")
                    m_pm = sch.op("vector",
                                  lambda e: e.tensor_mul(sb["p0tmp"][:], sb["p0eq"][:],
                                                         sb["p0sc"][:]), inc="sV")
                    sch.wait("vector", m_pm, force=True)
                    m_wj = sch.op("vector",
                                  lambda e: e.reduce_sum(c[:, 1:2], sb["p0tmp"][:], AX.X),
                                  inc="sV")
                    sch.wait("vector", m_wj, force=True)
                    if DEBUG and tile_abs == 0:
                        sch.op("vector", lambda e, jj=j: e.tensor_copy(
                            sb["dbg_rt_sb"][:, jj * 16:(jj + 1) * 16], sb["p0eq"][:]))
                        sch.op("vector", lambda e, jj=j: e.tensor_copy(
                            sb["dbg_rt_sb"][:, 64 + jj:65 + jj], c[:, 1:2]))
                    sch.op("vector",
                           lambda e: e.scalar_tensor_tensor(
                               sb["p0acc"][:], sb["p0eq"][:], c[:, 1:2], sb["p0acc"][:],
                               ALU.mult, ALU.add))
                    if j == 0:
                        m_wsum = sch.op("vector",
                                        lambda e: e.tensor_copy(c[:, 2:3], c[:, 1:2]),
                                        inc="sV")
                    else:
                        m_wsum = sch.op("vector",
                                        lambda e: e.tensor_add(c[:, 2:3], c[:, 2:3],
                                                               c[:, 1:2]), inc="sV")
                if DEBUG and tile_abs == 0:
                    sch.op("vector", lambda e: e.tensor_copy(
                        sb["dbg_rt_sb"][:, 68:69], c[:, 2:3]))
                    sch.op("vector", lambda e: e.tensor_copy(
                        sb["dbg_rt_sb"][:, 72:88], sb["p0acc"][:]))
                    sch.op("vector", lambda e: e.tensor_copy(
                        sb["dbg_rt_sb"][:, 96:112], sb["p0sc"][:]))
                sch.wait("vector", m_wsum, force=True)
                m_rn = sch.op("vector", lambda e: e.reciprocal(c[:, 3:4], c[:, 2:3]),
                              inc="sV")
                sch.wait("vector", m_rn, force=True)
                m_cb = sch.op("vector",
                              lambda e, ti=tile_abs: e.tensor_scalar(
                                  sb["comb"][:, ti * 16:(ti + 1) * 16], sb["p0acc"][:],
                                  c[:, 3:4], None, ALU.mult), inc="sV")
                if DEBUG and tile_abs == 0:
                    sch.op("vector", lambda e: e.tensor_copy(
                        sb["dbg_rt_sb"][:, 69:70], c[:, 3:4]))
                    mdr = sch.op("vector", lambda e: e.tensor_copy(
                        sb["dbg_rt_sb"][:, 70:71], c[:, 3:4]), inc="sV")
                    sch.wait("sync", mdr)
                    sch.dma("sync", lambda e: e.dma_start(ap["dbg_rt"],
                                                          sb["dbg_rt_sb"][:]), "ddbg")
                m_route_free = m_cb
        m_comb_done = sch.mark("sV")
        if DEBUG:
            ap["dbg_sc0"] = nc.dram_tensor("dbg_sc0", [128, 16], F32,
                                           kind="ExternalOutput").ap()
            ap["dbg_rt"] = nc.dram_tensor("dbg_rt", [128, 112], F32,
                                          kind="ExternalOutput").ap()
            t("dbg_rt_sb", [128, 112], F32, es_p0)
            ap["dbg_comb"] = nc.dram_tensor("dbg_comb", [128, NTILE * 16], F32,
                                            kind="ExternalOutput").ap()
            ap["dbg_stats"] = nc.dram_tensor("dbg_stats", [128, 64], F32,
                                             kind="ExternalOutput").ap()
            sch.wait("sync", m_comb_done)
            sch.wait("sync", m_p0a_v)
            sch.dma("sync", lambda e: e.dma_start(ap["dbg_comb"], sb["comb"][:]), "ddbg")
            sch.dma("sync", lambda e: e.dma_start(ap["dbg_stats"], sb["statsA"][:]), "ddbg")
        m_p0b_dma = sch.mark("dxt")

        m_p0_done = [sch.mark("sP"), sch.mark("sV"), sch.mark("sA")]
        es_p0.close()
        t("bc_cache", [128, NST * 512], F32, es_moe)
        t("wcol", [128, 2 * 16 * 128], BF16, es_moe)
        t("wdT", [128, 8 * DM], BF16, es_moe)
        t("he", [128, 8 * SUB], BF16, es_moe)
        t("silu_t", [128, 512], F32, es_moe)

        if DEBUG:
            ap["dbg_wc"] = nc.dram_tensor("dbg_wc", [128, 2048], BF16,
                                          kind="ExternalOutput").ap()
            ap["dbg_he"] = nc.dram_tensor("dbg_he", [128, 8 * SUB], BF16,
                                          kind="ExternalOutput").ap()
            ap["dbg_r0"] = nc.dram_tensor("dbg_r0", [128, 4 * DM], F32,
                                          kind="ExternalOutput").ap()

        m_dtl_free = [None, None]
        m_bcq_free = [None, None, None]
        m_xr_free = None
        m_bank4_free = m_bank45_free[0]
        m_bank56_free = [m_bank45_free[1], None]
        m_scr2_free = [None, None]
        m_cache_free = [None] * NST
        m_cache_rdy = [None] * NST
        m_sT_free = None
        m_ybuf_free = None
        m_ytp = {}
        m_prep_v = {}
        m_dtt_rd = None
        m_dtx_rd = None
        m_state_upd = None
        m_scan_first = True
        m_scan_dma = None

        def emit_sT_block(cp):
            nonlocal m_bank4_free, m_sT_free
            yb = sb["ybuf"]
            for dti in range(8):
                sch.wait("tensor", m_ytp.get(("yadd", cp)))
                sch.wait("tensor", m_bank4_free)
                tp(yb[:, dti * LC:dti * LC + 128], ps[4][:, 0:128], inc=False)
                m_tp2 = tp(yb[:, dti * LC + 128:(dti + 1) * LC], ps[4][:, 128:256])
                sch.wait("vector", m_tp2)
                sch.wait("vector", m_sT_free)
                m_evs = sch.op("vector",
                               lambda e: e.tensor_copy(sb["sTst"][:, 0:256],
                                                       ps[4][:, 0:256]), inc="sV")
                m_bank4_free = m_evs
                sch.wait("scalar", m_evs)
                for hh in range(2):
                    m_sT_free = sch.dma(
                        "scalar",
                        lambda e, c_=cp, d=dti, h_=hh: e.dma_start(
                            ap["sT_all"][c_ * LC + h_ * 128:c_ * LC + (h_ + 1) * 128,
                                         d * 128:(d + 1) * 128],
                            sb["sTst"][:, h_ * 128:(h_ + 1) * 128]), "dsT")
            m_ytp[("sT", cp)] = sch.mark("sP")

        def emit_scan_pe(ci):
            nonlocal m_xr_free, m_bank4_free, m_scan_first
            nonlocal m_bank7_free
            par = ci % 2
            sch.wait("sync", m_dtl_free[par])
            sch.wait("sync", m_p0a_dma[0])
            sch.wait("sync", m_p0a_dma[1])
            m_dtl = sch.dma("sync",
                            lambda e, c_=ci, p=par: e.dma_start(
                                sb["dtl"][:, p * LC:(p + 1) * LC],
                                ap["dtl_dram"][:, c_ * LC:(c_ + 1) * LC]),
                            "dtl0" if par == 0 else "dtl1")
            def bc_load(q):
                slot = (ci * 4 + q) % 3
                ssem = f"dbc{slot}"
                sch.wait("sync", m_bcq_free[slot])
                sch.dma("sync",
                        lambda e, c_=ci, qq=q, sl=slot: e.dma_start(
                            sb["bcrow"][0:1, sl * 2048:sl * 2048 + 1024].rearrange(
                                "o (n t) -> o n t", n=4),
                            ap["bc_dram"][qq * 4:(qq + 1) * 4,
                                          c_ * LC:(c_ + 1) * LC].rearrange(
                                "(o n) t -> o n t", o=1)), ssem)
                return sch.dma("sync",
                               lambda e, c_=ci, qq=q, sl=slot: e.dma_start(
                                   sb["bcrow"][0:1, sl * 2048 + 1024:(sl + 1) * 2048].rearrange(
                                       "o (n t) -> o n t", n=4),
                                   ap["bc_dram"][16 + qq * 4:16 + (qq + 1) * 4,
                                                 c_ * LC:(c_ + 1) * LC].rearrange(
                                       "(o n) t -> o n t", o=1)), ssem)

            m_bc_l = [bc_load(0), bc_load(1), bc_load(2), None]
            for n in range(NST):
                q = n // 4
                slot = (ci * 4 + q) % 3
                bnk = 5 + (n % 2)
                sch.wait("tensor", m_ones)
                sch.wait("tensor", m_bc_l[q])
                sch.wait("tensor", m_bank56_free[n % 2])
                mm(sb["ones_sb"][0:1, :],
                   sb["bcrow"][0:1, slot * 2048 + (n % 4) * LC:
                               slot * 2048 + (n % 4 + 1) * LC],
                   ps[bnk][:, 0:256], start=True, stop=True, inc=False)
                m_bc = mm(sb["ones_sb"][0:1, :],
                          sb["bcrow"][0:1, slot * 2048 + 1024 + (n % 4) * LC:
                                      slot * 2048 + 1024 + (n % 4 + 1) * LC],
                          ps[bnk][:, 256:512], start=True, stop=True)
                if n % 4 == 3:
                    m_bcq_free[slot] = m_bc
                    if n == 3:
                        m_bc_l[3] = bc_load(3)
                sch.wait("scalar", m_bc)
                sch.wait("scalar", m_cache_free[n])
                m_ev = sch.op("scalar",
                              lambda e, nn=n, b_=bnk: e.activation(
                                  sb["bc_cache"][:, nn * 512:(nn + 1) * 512],
                                  ps[b_][:, 0:512], AF.Identity), inc="sA")
                m_cache_rdy[n] = m_ev
                m_bank56_free[n % 2] = m_ev
            for g in range(2):
                sch.wait("sync", m_xr_free)
                mx = sch.dma("sync",
                             lambda e, c_=ci, gg=g: e.dma_start(
                                 sb["xr"].ap().rearrange("p (tt f) -> p tt f", tt=2),
                                 ap["xsdh"][c_ * LC:(c_ + 1) * LC,
                                            gg * 512:(gg + 1) * 512].rearrange(
                                     "(tt p) f -> p tt f", p=128)), "dxr")
                m_xn_l = []
                for tt in range(2):
                    sch.wait("scalar", mx)
                    if m_scan_first:
                        sch.wait("scalar", m_p0a_v)
                    til = ci * 2 + tt
                    m_l = sch.op("scalar",
                                 lambda e, t_=tt, ti=til: e.activation(
                                     sb["xr"][:, t_ * 512:(t_ + 1) * 512],
                                     sb["xr"][:, t_ * 512:(t_ + 1) * 512], AF.Identity,
                                     scale=sb["statsA"][:, 2 * ti:2 * ti + 1],
                                     bias=sb["statsA"][:, 2 * ti + 1:2 * ti + 2]),
                                 inc="sA")
                    m_xn_l.append(m_l)
                m_scan_first = False
                for j in range(4):
                    dti = g * 4 + j
                    blk = dti
                    sch.wait("tensor", m_xn_l[0])
                    sch.wait("tensor", m_xn_l[1])
                    sch.wait("tensor", m_bank7_free)
                    tp(sb["xr"][:, j * 128:(j + 1) * 128], ps[7][:, 0:128], inc=False)
                    m_tp = tp(sb["xr"][:, 512 + j * 128:512 + (j + 1) * 128],
                              ps[7][:, 128:256])
                    sch.wait("vector", m_tp)
                    m_ev = sch.op("vector",
                                  lambda e, jj=j: e.tensor_copy(
                                      sb["xnsc"][:, jj * LC:(jj + 1) * LC], ps[7][:, 0:256]),
                                  inc="sV")
                    m_bank7_free = m_ev
                    sch.wait("tensor", m_dtl)
                    sch.wait("tensor", m_bank4_free)
                    m_dmm = mm(sb["wdtT_sb"][:, dti * 128:(dti + 1) * 128],
                               sb["dtl"][:, par * LC:(par + 1) * LC],
                               ps[4][:, 0:256], start=True, stop=True)
                    sch.wait("scalar", m_dmm)
                    sch.wait("scalar", m_scr2_free[j % 2])
                    m_e1 = sch.op("scalar",
                                  lambda e, jj=j, d=dti: e.activation(
                                      sb["scr2"][:, (jj % 2) * LC:(jj % 2 + 1) * LC],
                                      ps[4][:, 0:256], AF.Exp,
                                      bias=sb["dtb_sb"][:, d:d + 1]), inc="sA")
                    m_bank4_free = m_e1
                    sch.wait("scalar", m_dtt_rd)
                    m_dt = sch.op("scalar",
                                  lambda e, jj=j, b_=blk: e.activation(
                                      sb["dtt"][:, b_ * LC:(b_ + 1) * LC],
                                      sb["scr2"][:, (jj % 2) * LC:(jj % 2 + 1) * LC],
                                      AF.Ln, bias=1.0), inc="sA")
                    m_scr2_free[j % 2] = m_dt
                    sch.wait("vector", m_dt)
                    sch.wait("vector", m_dtx_rd)
                    sch.op("vector",
                           lambda e, jj=j, b_=blk: e.tensor_mul(
                               sb["dtx"][:, b_ * LC:(b_ + 1) * LC],
                               sb["dtt"][:, b_ * LC:(b_ + 1) * LC],
                               sb["xnsc"][:, jj * LC:(jj + 1) * LC]))
                    sch.wait("vector", m_ytp.get(("sT", ci - 1)))
                    sch.op("vector",
                           lambda e, jj=j, b_=blk, d=dti: e.tensor_scalar(
                               sb["ybuf"][:, b_ * LC:(b_ + 1) * LC],
                               sb["xnsc"][:, jj * LC:(jj + 1) * LC],
                               sb["dvec_sb"][:, d:d + 1], None, ALU.mult))
                m_xr_free = sch.mark("sP")
            m_prep_v[ci] = sch.mark("sV")
            m_dtl_free[par] = sch.mark("sP")

        def emit_scan_dve(ci):
            nonlocal m_state_upd, m_dtt_rd, m_dtx_rd
            par = ci % 2
            last_dA_rd = None
            for g in range(2):
                for n in range(NST):
                    for j in range(4):
                        dti = g * 4 + j
                        blk = dti
                        jp = j % 2
                        sch.wait("scalar", m_scr2_free[jp])
                        m_dA = sch.op("scalar",
                                      lambda e, p_=jp, b_=blk, d=dti, nn=n: e.activation(
                                          sb["scr2"][:, p_ * LC:(p_ + 1) * LC],
                                          sb["dtt"][:, b_ * LC:(b_ + 1) * LC], AF.Exp,
                                          scale=sb["A_sb"][:, d * NST + nn:
                                                           d * NST + nn + 1]),
                                      inc="sA")
                        sch.wait("vector", m_cache_rdy[n])
                        sch.wait("vector", m_dA)
                        if ci == 0:
                            sch.wait("vector", m_state0)
                            sch.wait("vector", m_aneg)
                        sch.op("vector",
                               lambda e, jj=j, b_=blk, nn=n: e.tensor_mul(
                                   sb["ubuf"][:, jj * LC:(jj + 1) * LC],
                                   sb["dtx"][:, b_ * LC:(b_ + 1) * LC],
                                   sb["bc_cache"][:, nn * 512:nn * 512 + 256]))
                        m_sc2 = sch.op("vector",
                                       lambda e, jj=j, p_=jp, d=dti, nn=n:
                                       e.tensor_tensor_scan(
                                           sb["hbuf"][:, jj * LC:(jj + 1) * LC],
                                           sb["scr2"][:, p_ * LC:(p_ + 1) * LC],
                                           sb["ubuf"][:, jj * LC:(jj + 1) * LC],
                                           sb["state"][:, nn * NDT + d:nn * NDT + d + 1],
                                           ALU.mult, ALU.add), inc="sV")
                        m_scr2_free[jp] = m_sc2
                        sch.op("vector",
                               lambda e, jj=j, nn=n: e.tensor_mul(
                                   sb["ubuf"][:, jj * LC:(jj + 1) * LC],
                                   sb["hbuf"][:, jj * LC:(jj + 1) * LC],
                                   sb["bc_cache"][:, nn * 512 + 256:(nn + 1) * 512]))
                        m_ya = sch.op("vector",
                                      lambda e, jj=j, b_=blk: e.tensor_add(
                                          sb["ybuf"][:, b_ * LC:(b_ + 1) * LC],
                                          sb["ybuf"][:, b_ * LC:(b_ + 1) * LC],
                                          sb["ubuf"][:, jj * LC:(jj + 1) * LC]),
                                      inc="sV")
                    if g == 1:
                        m_cache_free[n] = m_ya
                    m_state_upd = sch.op("vector",
                                         lambda e, nn=n, gg=g: e.tensor_copy(
                                             sb["state"][:, nn * NDT + gg * 4:
                                                         nn * NDT + gg * 4 + 4],
                                             bass.AP(sb["hbuf"].ap().tensor,
                                                     sb["hbuf"].ap().offset + LC - 1,
                                                     [sb["hbuf"].ap().ap[0], [LC, 4]])),
                                         inc="sV")
            m_ytp[("yadd", ci)] = m_state_upd
            m_dtt_rd = sch.mark("sA")
            m_dtx_rd = sch.mark("sV")

        m_wcol_free = [None] * 2
        m_wdT_free = None
        m_xnT_free = None
        m_he_free = None
        m_ue_free = sch.mark("sA")
        m_silu_free = sch.mark("sA")
        m_silu_rd = [None, None]
        m_m2bank_free = [sch.mark("sA"), sch.mark("sA")]
        m_final_gate = []

        col_jobs = []
        for si_ in range(NSUB):
            for e2 in EORDER:
                for it_ in range(8):
                    for mi_ in range(2):
                        col_jobs.append((si_, e2, mi_, it_))
        col_loaded = {}
        nxt = {"i": 0}
        m_wcast_all = sch.mark("dW")

        def wcol_load(slot, e_i, m_i, it):
            src = ap["w16"][e_i, m_i].rearrange("(it p f) -> it p f", it=8, p=128)
            src = src[it]
            dst = sb["wcol"][:, slot * 2048:(slot + 1) * 2048]
            for mk in m_p0_done:
                sch.wait("scalar", mk)
            sch.wait("scalar", m_wcast_all)
            sch.wait("scalar", m_wcol_free[slot])
            return sch.dma("scalar", lambda e, d=dst, s=src: e.dma_start(d, s),
                           f"dwc{slot}")

        def ensure_col(idx):
            while nxt["i"] <= idx + 1 and nxt["i"] < len(col_jobs):
                _, e2, mi_, it_ = col_jobs[nxt["i"]]
                slot = nxt["i"] % 2
                col_loaded[nxt["i"]] = (slot, wcol_load(slot, e2, mi_, it_))
                nxt["i"] += 1

        job = {"i": 0}
        hold = {"m1last": None}
        m_xnT = None

        def emit_expert(si, e_i):
            nonlocal m_wdT_free, m_he_free, m_ue_free, m_silu_free
            nonlocal m_m2bank_free
            for mk in m_p0_done:
                sch.wait("scalar", mk)
            sch.wait("scalar", m_wcast_all)
            sch.wait("scalar", m_wdT_free)
            wd_marks = []
            for hh in range(2):
                src = ap["w16"][e_i, 2].rearrange("(p f) -> p f", p=128)
                src = src[:, hh * 8192:(hh + 1) * 8192]
                dst = sb["wdT"][:, hh * 8192:(hh + 1) * 8192]
                wd_marks.append(sch.dma("scalar",
                                        lambda e, d=dst, s=src: e.dma_start(d, s), "dwd"))
            m_ge = {}
            m_hev = None
            for it in range(8):
                for m_i in range(2):
                    ensure_col(job["i"])
                    slot, m_c = col_loaded[job["i"]]
                    job["i"] += 1
                    bank = m_i
                    sch.wait("tensor", m_c)
                    sch.wait("tensor", m_xnT)
                    if m_i == 0:
                        sch.wait("tensor", m_silu_free)
                    else:
                        sch.wait("tensor", m_ue_free)
                    for kt in range(16):
                        m_mm = mm(sb["wcol"][:, slot * 2048 + kt * 128:
                                             slot * 2048 + (kt + 1) * 128],
                                  sb["xnT"][:, kt * 512:(kt + 1) * 512],
                                  ps[bank][:, 0:512],
                                  start=(kt == 0), stop=(kt == 15))
                    m_wcol_free[slot] = m_mm
                    m_ge[(m_i, it)] = m_mm
                sch.wait("scalar", m_ge[(0, it)])
                sch.wait("scalar", m_silu_rd[0])
                m_si = sch.op("scalar",
                              lambda e: e.activation(sb["silu_t"][:, 0:512],
                                                     ps[0][:, 0:512], AF.Silu), inc="sA")
                m_silu_free = m_si
                sch.wait("vector", m_si)
                sch.wait("vector", m_ge[(1, it)])
                sch.wait("vector", m_he_free)
                m_hev = sch.op("vector",
                               lambda e, i_=it: e.tensor_mul(
                                   sb["he"][:, i_ * 512:(i_ + 1) * 512],
                                   sb["silu_t"][:, 0:512], ps[1][:, 0:512]), inc="sV")
                m_silu_rd[0] = m_hev
                m_ue_free = m_hev
            hold["m1last"] = m_ge[(1, 7)]
            m_mm2 = None
            m_rv = None
            for tt in range(4):
                tile_abs = si * 4 + tt
                for dc in range(4):
                    bank = 2 + (dc % 2)
                    sch.wait("tensor", m_hev)
                    sch.wait("tensor", wd_marks[1])
                    sch.wait("tensor", m_m2bank_free[dc % 2])
                    for kt in range(8):
                        m_mm2 = mm(sb["he"][:, kt * 512 + tt * 128:
                                            kt * 512 + (tt + 1) * 128],
                                   sb["wdT"][:, kt * 2048 + dc * 512:
                                             kt * 2048 + (dc + 1) * 512],
                                   ps[bank][:, 0:512],
                                   start=(kt == 0), stop=(kt == 7))
                    sch.wait("vector", m_mm2)
                    rsl = sb["routed"][:, tt * DM + dc * 512:tt * DM + (dc + 1) * 512]
                    if e_i == 16:
                        m_rv = sch.op("vector",
                                      lambda e, r=rsl, b_=bank: e.tensor_copy(
                                          r, ps[b_][:, 0:512]), inc="sV")
                    else:
                        sch.wait("vector", m_comb_done)
                        m_rv = sch.op("vector",
                                      lambda e, r=rsl, b_=bank, ta=tile_abs, ee=e_i:
                                      e.scalar_tensor_tensor(
                                          r, ps[b_][:, 0:512],
                                          sb["comb"][:, ta * 16 + ee:ta * 16 + ee + 1],
                                          r, ALU.mult, ALU.add), inc="sV")
                    m_m2bank_free[dc % 2] = m_rv
            if DEBUG and si == 0 and e_i == 16:
                sch.wait("sync", m_hev)
                sch.dma("sync", lambda e: e.dma_start(ap["dbg_he"], sb["he"][:]), "ddbg")
                sch.wait("sync", m_rv)
                sch.dma("sync", lambda e: e.dma_start(ap["dbg_r0"], sb["routed"][:]), "ddbg")
            m_wdT_free = m_mm2
            m_he_free = m_mm2
            return m_rv

        sTf = ap["sT_all"].rearrange("r c -> (r c)")
        agof = ap["ag_out"].rearrange("r c -> (r c)")

        def dyn_copy(e, dst, flat_src, slot):
            rv = e.value_load(sb["sel_sb"][0:1, slot:slot + 1])
            src = flat_src[bass.ts(rv, TOK * HD)].rearrange("(r c) -> r c", r=TOK)
            return e.dma_start(dst, src)

        def emit_ag():
            sch.wait("gpsimd", m_sel)
            sch.wait("gpsimd", m_scan_dma)
            m_agin = sch.dma("gpsimd", lambda e: dyn_copy(e, ap["ag_in"], sTf, 1), "dag")
            sch.wait("gpsimd", m_agin)
            sch.op("gpsimd",
                   lambda e: e.collective_compute(
                       "AllGather", ALU.bypass,
                       replica_groups=[[0, 1], [2, 3], [4, 5], [6, 7]],
                       ins=[ap["ag_in"].opt()], outs=[ap["ag_out"].opt()]),
                   inc="cc")
            m_cc = sch.op("gpsimd",
                          lambda e: e.collective_compute(
                              "AllGather", ALU.bypass,
                              replica_groups=[[0, 1], [2, 3], [4, 5], [6, 7]],
                              ins=[ap["agb_in"].opt()], outs=[ap["agb_out"].opt()]),
                          inc="cc")
            m_smine = sch.dma("gpsimd",
                              lambda e: dyn_copy(e, ap["smine_dram"], sTf, 0), "dag")
            sch.wait("gpsimd", m_cc)
            return m_smine

        m_smine = None
        slots = [(si_, e2) for si_ in range(NSUB) for e2 in EORDER]
        for s_i, (si, e_i) in enumerate(slots):
            if e_i == EORDER[0]:
                sch.wait("scalar", m_p0b_dma)
                sch.wait("scalar", m_xnT_free)
                m_xnT = sch.dma("scalar",
                                lambda e, s_=si: e.dma_start(
                                    sb["xnT"][:],
                                    ap["xnT_dram"][:, s_ * 16 * 512:
                                                   (s_ + 1) * 16 * 512]), "dxn")
            ci = s_i // 2
            on_chunk = (s_i % 2 == 0) and ci < NCHUNK
            if on_chunk:
                if ci >= 1:
                    emit_sT_block(ci - 1)
                emit_scan_pe(ci)
            elif s_i == 2 * NCHUNK:
                emit_sT_block(NCHUNK - 1)
                m_scan_dma = sch.mark("dsT")
                m_smine = emit_ag()
            m_rv = emit_expert(si, e_i)
            if on_chunk:
                emit_scan_dve(ci)
                if DEBUG and s_i == 0:
                    for nm2, wd2, src2 in (("dbg_bcc", NST * 512, "bc_cache"),
                                           ("dbg_dtt", 8 * LC, "dtt"),
                                           ("dbg_dtx", 8 * LC, "dtx"),
                                           ("dbg_yb", 8 * LC, "ybuf")):
                        ap[nm2] = nc.dram_tensor(nm2, [128, wd2], F32,
                                                 kind="ExternalOutput").ap()
                        sch.wait("sync", sch.mark("sV"))
                        sch.wait("sync", sch.mark("sA"))
                        sch.dma("sync", lambda e, d=ap[nm2], s=sb[src2]:
                                e.dma_start(d, s[:]), "ddbg")
            if e_i == EORDER[-1]:
                m_xnT_free = hold["m1last"]
                sch.wait("scalar", m_rv)
                m_rsp = sch.dma("scalar",
                                lambda e, s_=si: e.dma_start(
                                    ap["routed_dram"][:, s_ * 4 * DM:(s_ + 1) * 4 * DM],
                                    sb["routed"][:]), "drs")
                m_final_gate.append(m_rsp)
                m_m2bank_free = [m_rsp, m_rsp]

        sch.wait("gpsimd", m_final_gate[-1])
        m_soth = sch.dma("gpsimd", lambda e: dyn_copy(e, ap["sother_dram"], agof, 2), "dag")

        es_moe.close()
        es_scan.close()
        t("ffin", [128, 4 * DM], F32)
        t("rbuf", [128, 2 * DM], F32)
        sfin = sb["ffin"][:, 0 * DM:1 * DM]
        xfin = sb["ffin"][:, 1 * DM:2 * DM]
        gbuf = sb["ffin"][:, 2 * DM:3 * DM]
        tmpo = sb["ffin"][:, 3 * DM:4 * DM]
        m_ffree = None
        m_gbuf_free = None
        m_rb_free = [None, None]
        for jt in range(NTILE):
            rp = jt % 2
            sch.wait("sync", m_final_gate[-1])
            sch.wait("sync", m_smine)
            sch.wait("sync", m_soth)
            sch.wait("sync", m_ffree)
            m1 = sch.dma("sync", lambda e, j=jt: e.dma_start(
                sfin[:, 0:HD], ap["smine_dram"][j * 128:(j + 1) * 128, :]), "dfin")
            m2 = sch.dma("sync", lambda e, j=jt: e.dma_start(
                sfin[:, HD:DM], ap["sother_dram"][j * 128:(j + 1) * 128, :]), "dfin")
            m3 = sch.dma("sync", lambda e, j=jt: e.dma_start(
                xfin, ap["xmine"][j * 128:(j + 1) * 128, :]), "dfin")
            sch.wait("sync", m_rb_free[rp])
            m4 = sch.dma("sync", lambda e, j=jt, p=rp: e.dma_start(
                sb["rbuf"][:, p * DM:(p + 1) * DM],
                ap["routed_dram"][:, j * DM:(j + 1) * DM]), "dfin")
            sch.wait("scalar", m4)
            sch.wait("scalar", m_gbuf_free)
            m_g = sch.op("scalar",
                         lambda e, p=rp: e.activation(
                             gbuf, sb["rbuf"][:, p * DM:(p + 1) * DM], AF.Silu), inc="sA")
            m_rb_free[rp] = m_g
            sch.wait("vector", m_g)
            sch.wait("vector", m1)
            sch.wait("vector", m2)
            sch.wait("vector", m3)
            sch.op("vector", lambda e: e.tensor_mul(tmpo, sfin, gbuf))
            m_o = sch.op("vector", lambda e: e.tensor_add(gbuf, tmpo, xfin), inc="sV")
            m_ffree = m_o
            sch.wait("sync", m_o)
            m_out = sch.dma("sync", lambda e, j=jt: e.dma_start(
                ap["out"][j * 128:(j + 1) * 128, :], gbuf), "dout")
            m_gbuf_free = m_out
        sch.wait("sync", sch.mark("dout"))
        sch.wait("sync", sch.mark("dfin"))
        sch.wait("sync", sch.mark("ddbg"))
        sch.wait("gpsimd", sch.mark("dag"))
        sch.wait("scalar", sch.mark("drs"))

        block = es.enter_context(nc.Block())

        def emit(engname):
            def f(eng):
                for st in sch.steps[engname]:
                    if st[0] == "w":
                        eng.wait_ge(sems[st[1]], st[2])
                    else:
                        inst = st[1](eng)
                        if st[2] is not None:
                            inst.then_inc(sems[st[2]], st[3])
            return f

        block.tensor(emit("tensor"))
        block.vector(emit("vector"))
        block.scalar(emit("scalar"))
        block.gpsimd(emit("gpsimd"))
        block.sync(emit("sync"))

    return nc


_cache = {}


def kernel(**inputs):
    x = np.asarray(inputs["L0_attn_x"])
    xpw = np.asarray(inputs["L0_attn_x_proj_w"])
    dtw = np.asarray(inputs["L0_attn_dt_proj_w"])
    dtb = np.asarray(inputs["L0_attn_dt_proj_b"])
    alog = np.asarray(inputs["L0_attn_A_log"])
    dv = np.asarray(inputs["L0_attn_D"])
    gw = np.asarray(inputs["L0_ffn_gate_w"])
    eb = np.asarray(inputs["L0_ffn_expert_bias"])
    wg = np.asarray(inputs["L0_ffn_w_gate"])
    wu = np.asarray(inputs["L0_ffn_w_up"])
    wd = np.asarray(inputs["L0_ffn_w_down"])
    swg = np.asarray(inputs["L0_ffn_sw_gate"])
    swu = np.asarray(inputs["L0_ffn_sw_up"])
    swd = np.asarray(inputs["L0_ffn_sw_down"])

    if "nc" not in _cache:
        _cache["nc"] = build_program()
    nc = _cache["nc"]

    C = np.ascontiguousarray

    def roll_d(a, k, axis):
        return np.roll(a, -k * HD, axis=axis) if k else a

    wgT = np.concatenate([wg.transpose(0, 2, 1), swg.T[None]], 0)
    wuT = np.concatenate([wu.transpose(0, 2, 1), swu.T[None]], 0)
    wdT = np.concatenate([wd.transpose(0, 2, 1), swd.T[None]], 0)

    def arr_m1(w):
        return w.reshape(NEXP, 16, 128, 8, 128).transpose(0, 3, 2, 1, 4).reshape(
            NEXP, DM, II)

    def arr_m2(w):
        return w.reshape(NEXP, 8, 128, DM).transpose(0, 2, 1, 3).reshape(NEXP, II, DM)
    xpT = xpw.T
    gwT = gw.T
    identf = np.eye(128, dtype=np.float32)
    ebias = C(np.broadcast_to(eb[None, :], (128, 16)).astype(np.float32))

    per_k = {}
    for k in (0, 1):
        dh = slice(k * HD, (k + 1) * HD)
        per_k[k] = dict(
            wg=C(arr_m1(roll_d(wgT, k, 1))),
            wu=C(arr_m1(roll_d(wuT, k, 1))),
            wd=C(arr_m2(roll_d(wdT, k, 2))),
            xpT=C(roll_d(xpT, k, 0)),
            gw=C(roll_d(gwT, k, 0)),
            wdtT=C(dtw[dh].T),
            dtb=C(dtb[dh].reshape(NDT, 128).T),
            alog=C(alog[dh].reshape(NDT, 128, NST).transpose(1, 0, 2).reshape(128, NDT * NST)),
            dvec=C(dv[dh].reshape(NDT, 128).T),
            sel=np.array([[k, 1 - k, 1 - k, 0, 0, 0, 0, 0]], np.int32),
        )

    in_maps = []
    for core in range(8):
        b, k = core // 2, core % 2
        xb = x[b]
        d = dict(per_k[k])
        d["xscan"] = C(roll_d(xb, k, 1))
        d["xsdh"] = C(xb[:, k * HD:(k + 1) * HD])
        d["xmine"] = C(roll_d(xb[k * TOK:(k + 1) * TOK], k, 1))
        d["ebias"] = ebias
        d["identf"] = identf
        in_maps.append(d)

    res = run_bass_kernel_spmd(nc, in_maps, core_ids=list(range(8)))
    out = np.empty((NB, L, DM), np.float32)
    for core in range(8):
        b, k = core // 2, core % 2
        o = np.asarray(res.results[core]["out"])
        out[b, k * TOK:(k + 1) * TOK] = np.roll(o, k * HD, axis=1) if k else o
    return out


# version 08 (final)
# speedup vs baseline: 1.0141x; 1.0141x over previous
import contextlib
import numpy as np
import ml_dtypes
import concourse.bass as bass
import concourse.mybir as mybir
from concourse.bass_utils import run_bass_kernel_spmd

F32 = mybir.dt.float32
BF16 = mybir.dt.bfloat16
I32 = mybir.dt.int32
AF = mybir.ActivationFunctionType
ALU = mybir.AluOpType
AX = mybir.AxisListType

NB, L, DM, HD, NST, RK, NEXP, II = 4, 4096, 2048, 1024, 16, 128, 17, 1024
TOK = 2048
SUB = 512
NSUB = TOK // SUB
LC = 256
NCHUNK = L // LC
NDT = HD // 128
LN_EPS = 1e-5
NTILE = TOK // 128
DEBUG = False

ENGS = ("tensor", "vector", "scalar", "gpsimd", "sync")
EORDER = [16] + list(range(16))


class Sch:
    def __init__(self):
        self.steps = {e: [] for e in ENGS}
        self.ct = {}
        self.last_wait = {e: {} for e in ENGS}

    def op(self, eng, fn, inc=None, n=1):
        self.steps[eng].append(("o", fn, inc, n))
        if inc is not None:
            self.ct[inc] = self.ct.get(inc, 0) + n
            return (inc, self.ct[inc], eng)
        return None

    def dma(self, eng, fn, sem):
        return self.op(eng, fn, inc=sem, n=16)

    def mark(self, sem):
        return (sem, self.ct.get(sem, 0), None)

    def wait(self, eng, mark, force=False):
        if mark is None:
            return
        sem, val, peng = mark
        if val <= 0:
            return
        if peng == eng and not force:
            return
        lw = self.last_wait[eng]
        if lw.get(sem, -1) >= val:
            return
        lw[sem] = val
        self.steps[eng].append(("w", sem, val))


def build_program():
    nc = bass.Bass("TRN2", target_bir_lowering=False)
    ap = {}

    def din(name, shape, dt):
        ap[name] = nc.dram_tensor(name, shape, dt, kind="ExternalInput").ap()

    def dout(name, shape, dt):
        ap[name] = nc.dram_tensor(name, shape, dt, kind="ExternalOutput").ap()

    def dint(name, shape, dt, dbg=False):
        kind = "ExternalOutput" if (DEBUG and dbg) else "Internal"
        ap[name] = nc.dram_tensor(name, shape, dt, kind=kind).ap()

    din("xscan", [L, DM], F32)
    din("xsdh", [L, HD], F32)
    din("xmine", [TOK, DM], F32)
    din("xpT", [DM, 160], F32)
    din("wdtT", [RK, HD], F32)
    din("dtb", [128, NDT], F32)
    din("alog", [128, NDT * NST], F32)
    din("dvec", [128, NDT], F32)
    din("gw", [DM, 16], F32)
    din("ebias", [128, 16], F32)
    din("wg", [NEXP, DM, II], F32)
    din("wu", [NEXP, DM, II], F32)
    din("wd", [NEXP, II, DM], F32)
    din("identf", [128, 128], F32)
    din("sel", [1, 8], I32)
    dout("out", [TOK, DM], F32)

    dint("w16", [NEXP, 3, DM * II], BF16)
    dint("dtl_dram", [RK, L], BF16, dbg=True)
    dint("bc_dram", [32, L], F32, dbg=True)
    dint("xnT_dram", [128, NSUB * 16 * SUB], BF16, dbg=True)
    dint("sT_all", [L, HD], F32, dbg=True)
    dint("ag_in", [TOK, HD], F32)
    dint("ag_out", [2 * TOK, HD], F32)
    dint("agb_in", [2, 64], F32)
    dint("agb_out", [4, 64], F32)
    dint("smine_dram", [TOK, HD], F32, dbg=True)
    dint("sother_dram", [TOK, HD], F32, dbg=True)
    dint("routed_dram", [128, NTILE * DM], F32, dbg=True)

    sch = Sch()
    es = contextlib.ExitStack()
    es_scan = contextlib.ExitStack()
    es_p0 = contextlib.ExitStack()
    es_moe = contextlib.ExitStack()

    with es:
        sb = {}

        def t(name, shape, dt, stack=None):
            sb[name] = (stack or es).enter_context(nc.sbuf_tensor(name, shape, dt))
            return sb[name]

        t("xnT", [128, 16 * SUB], BF16)
        t("routed", [128, (SUB // 128) * DM], F32)
        t("comb", [128, NTILE * 16], F32)
        t("statsA", [128, 2 * (L // 128)], F32)
        t("A_sb", [128, NDT * NST], F32)
        t("wdtT_sb", [128, HD], BF16)
        t("ebias_sb", [128, 16], F32)
        t("dvec_sb", [128, NDT], F32)
        t("dtb_sb", [128, NDT], F32)
        t("identf_sb", [128, 128], F32)
        t("ones_sb", [1, 128], F32)
        t("sel_sb", [1, 8], I32)
        t("state", [128, NST * NDT], F32)
        t("stats1", [128, 12], F32)
        t("epscol", [128, 1], F32)
        t("negcol", [128, 1], F32)

        t("xr", [128, 2 * 512], F32, es_scan)
        t("xnsc", [128, 4 * LC], F32, es_scan)
        t("dtt", [128, 8 * LC], F32, es_scan)
        t("dtx", [128, 8 * LC], F32, es_scan)
        t("ybuf", [128, 8 * LC], F32, es_scan)
        t("hbuf", [128, 4 * LC], F32, es_scan)
        t("ubuf", [128, 4 * LC], F32, es_scan)
        t("scr2", [128, 2 * LC], F32, es_scan)
        t("dtl", [128, 2 * LC], BF16, es_scan)
        t("bcrow", [1, 3 * 2048], F32, es_scan)
        t("sTst", [128, LC], F32, es_scan)

        t("xpT_sb", [128, 16 * 160], BF16, es_p0)
        t("gw_sb", [128, 16 * 16], F32, es_p0)
        t("p0x", [128, 2 * DM], F32, es_p0)
        t("p0sq", [128, DM], F32, es_p0)
        t("p0xn", [128, 4 * DM], F32, es_p0)
        t("p0bt", [128, 512], BF16, es_p0)
        t("p0ft", [128, 512], F32, es_p0)
        t("p0dtl", [128, 2 * 512], BF16, es_p0)
        t("p0bc", [32, 2 * 512], F32, es_p0)
        t("p0sc", [128, 16], F32, es_p0)
        t("p0key", [128, 16], F32, es_p0)
        t("p0eq", [128, 16], F32, es_p0)
        t("p0acc", [128, 16], F32, es_p0)
        t("p0tmp", [128, 16], F32, es_p0)
        t("p0w", [128, 4], F32, es_p0)

        ps = {}
        for i in range(8):
            ps[i] = es.enter_context(nc.psum_tensor(f"pb{i}", [128, 512], F32))

        sems = {}
        for s_ in ("dSm", "dXa", "dXb", "dxr", "dtl0", "dtl1", "dbc0", "dbc1", "dbc2",
                   "dfin", "dout", "ddbg", "dSp0", "dSp1", "dxt", "dsT",
                   "dwc0", "dwc1", "dwc2", "dwc3", "dwd", "dxn", "drs",
                   "dW", "dG", "dag", "cc", "sP", "sV", "sA"):
            sems[s_] = es.enter_context(nc.semaphore(s_))

        def mm(lhsT, rhs, out, start, stop, inc=True):
            return sch.op("tensor",
                          lambda e, a=out, b=lhsT, c=rhs, s1=start, s2=stop:
                          e.matmul(a, b, c, start=s1, stop=s2),
                          inc="sP" if inc else None)

        def tp(in_ap, out_ap, inc=True):
            return sch.op("tensor",
                          lambda e, a=out_ap, b=in_ap: e.transpose(a, b, sb["identf_sb"][:]),
                          inc="sP" if inc else None)

        def sload(dst_ap, src_ap, sem="dSm", queue="sync"):
            return sch.dma(queue, lambda e, d=dst_ap, s=src_ap: e.dma_start(d, s), sem)

        m_xpT = sload(sb["xpT_sb"].ap().rearrange("p (kt m) -> p kt m", kt=16),
                      ap["xpT"].rearrange("(kt p) m -> p kt m", p=128),
                      sem="dG", queue="gpsimd")
        m_wdt = sload(sb["wdtT_sb"][:], ap["wdtT"], sem="dG", queue="gpsimd")
        m_gw = sload(sb["gw_sb"].ap().rearrange("p (dt e) -> p dt e", dt=16),
                     ap["gw"].rearrange("(dt p) e -> p dt e", p=128))
        m_eb = sload(sb["ebias_sb"][:], ap["ebias"])
        m_dv = sload(sb["dvec_sb"][:], ap["dvec"])
        m_db = sload(sb["dtb_sb"][:], ap["dtb"])
        m_alog = sload(sb["A_sb"][:], ap["alog"])
        m_idf = sload(sb["identf_sb"][:], ap["identf"])
        m_sel = sload(sb["sel_sb"][:], ap["sel"])

        m_total_sm = sch.mark("dSm")
        m_xpT = m_wdt = sch.mark("dG")
        m_gw = m_eb = m_dv = m_db = m_alog = m_idf = m_sel = m_total_sm
        m_ones = sch.op("vector", lambda e: e.memset(sb["ones_sb"][:], 1.0), inc="sV")
        m_eps = sch.op("vector", lambda e: e.memset(sb["epscol"][:], LN_EPS), inc="sV")
        m_neg = sch.op("vector", lambda e: e.memset(sb["negcol"][:], -1e30), inc="sV")
        m_state0 = sch.op("vector", lambda e: e.memset(sb["state"][:], 0.0), inc="sV")

        sch.wait("scalar", m_alog)
        m_aexp = sch.op("scalar",
                        lambda e: e.activation(sb["A_sb"][:], sb["A_sb"][:], AF.Exp),
                        inc="sA")
        sch.wait("vector", m_aexp)
        m_aneg = sch.op("vector",
                        lambda e: e.tensor_scalar_mul(sb["A_sb"][:], sb["A_sb"][:], -1.0),
                        inc="sV")

        wsrc = {0: ap["wg"], 1: ap["wu"], 2: ap["wd"]}
        wcast_mark = {}
        for e_i in EORDER:
            for m_i in range(3):
                csrc = wsrc[m_i][e_i].rearrange("a b -> (a b)").rearrange("(o f) -> o f", o=1)
                cdst = ap["w16"][e_i, m_i].rearrange("(o f) -> o f", o=1)
                wcast_mark[(e_i, m_i)] = sch.dma(
                    "gpsimd", lambda e, d=cdst, s=csrc: e.dma_start(d, s), "dW")

        state_h = {"prev_xn": None}

        def stats_and_xn(src_ap_t, row0, xn_out_ap, xt_slot, rstd, nmr, m_xt_free):
            c = sb["stats1"]
            sch.wait("sync", m_xt_free)
            xts = sb["p0x"][:, xt_slot * DM:(xt_slot + 1) * DM]
            mload = sch.dma("sync",
                            lambda e, d=xts, s=src_ap_t[row0:row0 + 128, :]:
                            e.dma_start(d, s), "dXa" if xt_slot == 0 else "dXb")
            sch.wait("vector", mload)
            sch.wait("vector", state_h["prev_xn"])

            def vop(fn):
                mk = sch.op("vector", fn, inc="sV")
                sch.wait("vector", mk, force=True)
                return mk

            vop(lambda e, x=xts: e.reduce_sum(c[:, 0:1], x, AX.X))
            vop(lambda e, x=xts: e.tensor_mul(sb["p0sq"][:], x, x))
            vop(lambda e: e.reduce_sum(c[:, 1:2], sb["p0sq"][:], AX.X))
            vop(lambda e: e.tensor_scalar_mul(c[:, 2:3], c[:, 0:1], 1.0 / DM))
            vop(lambda e: e.tensor_scalar_mul(c[:, 3:4], c[:, 1:2], 1.0 / DM))
            vop(lambda e: e.tensor_mul(c[:, 4:5], c[:, 2:3], c[:, 2:3]))
            m_var = vop(lambda e: e.tensor_sub(c[:, 5:6], c[:, 3:4], c[:, 4:5]))
            sch.wait("scalar", m_var)
            sch.wait("scalar", m_eps)
            m_ln = sch.op("scalar",
                          lambda e: e.activation(c[:, 6:7], c[:, 5:6], AF.Ln,
                                                 bias=sb["epscol"][:]), inc="sA")
            sch.wait("scalar", m_ln, force=True)
            m_r = sch.op("scalar",
                         lambda e, r=rstd: e.activation(r, c[:, 6:7], AF.Exp, scale=-0.5),
                         inc="sA")
            sch.wait("vector", m_r)
            m_nm = vop(lambda e, r=rstd, o=nmr:
                       e.scalar_tensor_tensor(o, c[:, 2:3], -1.0, r, ALU.mult, ALU.mult))
            sch.wait("scalar", m_nm)
            m_xn = sch.op("scalar",
                          lambda e, o=xn_out_ap, x=xts, r=rstd, n=nmr:
                          e.activation(o, x, AF.Identity, scale=r, bias=n),
                          inc="sA")
            state_h["prev_xn"] = m_xn
            return m_xn

        sch.wait("tensor", m_idf)
        sch.wait("tensor", m_xpT)
        sch.wait("tensor", m_gw)
        sch.wait("tensor", m_wdt)
        sch.wait("scalar", m_db)
        sch.wait("vector", m_dv)
        m_xt_free = [None, None]
        m_p0xn_free = [None] * 4
        m_bank7_free = None
        m_bank45_free = [None, None]
        m_p0stage_free = [None, None]
        m_p0bt_free = None

        for g in range(L // 512):
            m_xn_tiles = []
            for tt in range(4):
                ti = g * 4 + tt
                sch.wait("scalar", m_p0xn_free[tt])
                m_xn = stats_and_xn(ap["xscan"], g * 512 + tt * 128,
                                    sb["p0xn"][:, tt * DM:(tt + 1) * DM], tt % 2,
                                    sb["statsA"][:, 2 * ti:2 * ti + 1],
                                    sb["statsA"][:, 2 * ti + 1:2 * ti + 2],
                                    m_xt_free[tt % 2])
                m_xt_free[tt % 2] = sch.mark("sA")
                m_xn_tiles.append(m_xn)
            for dt_i in range(16):
                for tt in range(4):
                    sch.wait("tensor", m_xn_tiles[tt])
                sch.wait("tensor", m_bank7_free)
                for tt in range(4):
                    m_tp = tp(sb["p0xn"][:, tt * DM + dt_i * 128:tt * DM + (dt_i + 1) * 128],
                              ps[7][:, tt * 128:(tt + 1) * 128], inc=(tt == 3))
                sch.wait("vector", m_tp)
                sch.wait("vector", m_p0bt_free)
                m_ev = sch.op("vector",
                              lambda e: e.tensor_copy(sb["p0bt"][:], ps[7][:]),
                              inc="sV")
                m_bank7_free = m_ev
                sch.wait("tensor", m_ev)
                if dt_i == 0:
                    sch.wait("tensor", m_bank45_free[0])
                    sch.wait("tensor", m_bank45_free[1])
                mm(sb["xpT_sb"][:, dt_i * 160:dt_i * 160 + 128], sb["p0bt"][:],
                   ps[4][:, 0:512], start=(dt_i == 0), stop=(dt_i == 15))
                m_xdbl = mm(sb["xpT_sb"][:, dt_i * 160 + 128:dt_i * 160 + 160],
                            sb["p0bt"][:], ps[5][0:32, 0:512],
                            start=(dt_i == 0), stop=(dt_i == 15))
                m_p0bt_free = m_xdbl
            for tt in range(4):
                m_p0xn_free[tt] = sch.mark("sP")
            par = g % 2
            sch.wait("vector", m_xdbl)
            sch.wait("vector", m_p0stage_free[par])
            m_e1 = sch.op("vector",
                          lambda e, p=par: e.tensor_copy(
                              sb["p0dtl"][:, p * 512:(p + 1) * 512], ps[4][:]), inc="sV")
            m_e2 = sch.op("vector",
                          lambda e, p=par: e.tensor_copy(
                              sb["p0bc"][0:32, p * 512:(p + 1) * 512], ps[5][0:32, :]),
                          inc="sV")
            m_bank45_free = [m_e1, m_e2]
            sch.wait("scalar", m_e2)
            sch.dma("scalar",
                    lambda e, p=par, gg=g: e.dma_start(
                        ap["dtl_dram"][:, gg * 512:(gg + 1) * 512],
                        sb["p0dtl"][:, p * 512:(p + 1) * 512]), "dSp0" if par == 0 else "dSp1")
            m_d2 = sch.dma("scalar",
                           lambda e, p=par, gg=g: e.dma_start(
                               ap["bc_dram"][:, gg * 512:(gg + 1) * 512],
                               sb["p0bc"][0:32, p * 512:(p + 1) * 512]),
                           "dSp0" if par == 0 else "dSp1")
            m_p0stage_free[par] = m_d2
        m_p0a_dma = [sch.mark("dSp0"), sch.mark("dSp1")]
        m_p0a_v = sch.mark("sV")

        m_logit_free = [None] * 4
        m_stagef_free = None
        m_xnTst_free = m_p0bt_free
        m_route_free = None

        for g in range(NSUB):
            m_xn_tiles = []
            for tt in range(4):
                sch.wait("scalar", m_p0xn_free[tt])
                m_xn = stats_and_xn(ap["xmine"], g * 512 + tt * 128,
                                    sb["p0xn"][:, tt * DM:(tt + 1) * DM], tt % 2,
                                    sb["stats1"][:, 8:9], sb["stats1"][:, 9:10],
                                    m_xt_free[tt % 2])
                m_xt_free[tt % 2] = sch.mark("sA")
                m_xn_tiles.append(m_xn)
            for dt_i in range(16):
                for tt in range(4):
                    sch.wait("tensor", m_xn_tiles[tt])
                sch.wait("tensor", m_bank7_free)
                for tt in range(4):
                    m_tp = tp(sb["p0xn"][:, tt * DM + dt_i * 128:tt * DM + (dt_i + 1) * 128],
                              ps[7][:, tt * 128:(tt + 1) * 128], inc=(tt == 3))
                sch.wait("vector", m_tp)
                sch.wait("vector", m_xnTst_free)
                m_evb = sch.op("vector",
                               lambda e: e.tensor_copy(sb["p0bt"][:], ps[7][:]), inc="sV")
                sch.wait("vector", m_stagef_free)
                m_evf = sch.op("vector",
                               lambda e: e.tensor_copy(sb["p0ft"][:], ps[7][:]), inc="sV")
                m_bank7_free = m_evf
                sch.wait("scalar", m_evb)
                m_xnTst_free = sch.dma(
                    "scalar",
                    lambda e, gg=g, d=dt_i: e.dma_start(
                        ap["xnT_dram"][:, gg * 16 * 512 + d * 512:gg * 16 * 512 + (d + 1) * 512],
                        sb["p0bt"][:]), "dxt")
                sch.wait("tensor", m_evf)
                for tt in range(4):
                    if dt_i == 0:
                        sch.wait("tensor", m_logit_free[tt])
                    m_lg = mm(sb["p0ft"][:, tt * 128:(tt + 1) * 128],
                              sb["gw_sb"].ap().rearrange("p (dt e) -> p dt e", dt=16)[:, dt_i, :],
                              ps[tt][:, 0:16], start=(dt_i == 0), stop=(dt_i == 15))
                m_stagef_free = m_lg
            for tt in range(4):
                tile_abs = g * 4 + tt
                sch.wait("scalar", m_lg)
                sch.wait("scalar", m_route_free)
                sch.wait("vector", m_eb)
                m_sc0 = sch.op("scalar",
                               lambda e, b=tt: e.activation(sb["p0tmp"][:], ps[b][:, 0:16],
                                                            AF.Exp, scale=-1.0), inc="sA")
                sch.op("scalar",
                       lambda e: e.activation(sb["p0eq"][:], sb["p0tmp"][:], AF.Ln,
                                              bias=1.0))
                m_sc = sch.op("scalar",
                              lambda e: e.activation(sb["p0sc"][:], sb["p0eq"][:], AF.Exp,
                                                     scale=-1.0), inc="sA")
                m_logit_free[tt] = m_sc0
                if DEBUG and tile_abs == 0:
                    sch.wait("sync", m_sc)
                    sch.dma("sync", lambda e: e.dma_start(ap["dbg_sc0"], sb["p0sc"][:]), "ddbg")
                sch.wait("vector", m_sc)
                m_ka = sch.op("vector",
                              lambda e: e.tensor_add(sb["p0key"][:], sb["p0sc"][:],
                                                     sb["ebias_sb"][:]), inc="sV")
                sch.op("vector", lambda e: e.memset(sb["p0acc"][:], 0.0))
                c = sb["p0w"]
                for j in range(4):
                    sch.wait("vector", m_ka, force=True)
                    m_mj = sch.op("vector",
                                  lambda e: e.reduce_max(c[:, 0:1], sb["p0key"][:], AX.X),
                                  inc="sV")
                    sch.wait("vector", m_mj, force=True)
                    sch.op("vector",
                           lambda e: e.tensor_scalar(sb["p0eq"][:], sb["p0key"][:],
                                                     c[:, 0:1], None, ALU.is_equal))
                    sch.wait("vector", m_neg)
                    m_ka = sch.op("vector",
                                  lambda e: e.scalar_tensor_tensor(
                                      sb["p0key"][:], sb["p0eq"][:], sb["negcol"][:],
                                      sb["p0key"][:], ALU.mult, ALU.add), inc="sV")
                    m_pm = sch.op("vector",
                                  lambda e: e.tensor_mul(sb["p0tmp"][:], sb["p0eq"][:],
                                                         sb["p0sc"][:]), inc="sV")
                    sch.wait("vector", m_pm, force=True)
                    m_wj = sch.op("vector",
                                  lambda e: e.reduce_sum(c[:, 1:2], sb["p0tmp"][:], AX.X),
                                  inc="sV")
                    sch.wait("vector", m_wj, force=True)
                    if DEBUG and tile_abs == 0:
                        sch.op("vector", lambda e, jj=j: e.tensor_copy(
                            sb["dbg_rt_sb"][:, jj * 16:(jj + 1) * 16], sb["p0eq"][:]))
                        sch.op("vector", lambda e, jj=j: e.tensor_copy(
                            sb["dbg_rt_sb"][:, 64 + jj:65 + jj], c[:, 1:2]))
                    sch.op("vector",
                           lambda e: e.scalar_tensor_tensor(
                               sb["p0acc"][:], sb["p0eq"][:], c[:, 1:2], sb["p0acc"][:],
                               ALU.mult, ALU.add))
                    if j == 0:
                        m_wsum = sch.op("vector",
                                        lambda e: e.tensor_copy(c[:, 2:3], c[:, 1:2]),
                                        inc="sV")
                    else:
                        m_wsum = sch.op("vector",
                                        lambda e: e.tensor_add(c[:, 2:3], c[:, 2:3],
                                                               c[:, 1:2]), inc="sV")
                if DEBUG and tile_abs == 0:
                    sch.op("vector", lambda e: e.tensor_copy(
                        sb["dbg_rt_sb"][:, 68:69], c[:, 2:3]))
                    sch.op("vector", lambda e: e.tensor_copy(
                        sb["dbg_rt_sb"][:, 72:88], sb["p0acc"][:]))
                    sch.op("vector", lambda e: e.tensor_copy(
                        sb["dbg_rt_sb"][:, 96:112], sb["p0sc"][:]))
                sch.wait("vector", m_wsum, force=True)
                m_rn = sch.op("vector", lambda e: e.reciprocal(c[:, 3:4], c[:, 2:3]),
                              inc="sV")
                sch.wait("vector", m_rn, force=True)
                m_cb = sch.op("vector",
                              lambda e, ti=tile_abs: e.tensor_scalar(
                                  sb["comb"][:, ti * 16:(ti + 1) * 16], sb["p0acc"][:],
                                  c[:, 3:4], None, ALU.mult), inc="sV")
                if DEBUG and tile_abs == 0:
                    sch.op("vector", lambda e: e.tensor_copy(
                        sb["dbg_rt_sb"][:, 69:70], c[:, 3:4]))
                    mdr = sch.op("vector", lambda e: e.tensor_copy(
                        sb["dbg_rt_sb"][:, 70:71], c[:, 3:4]), inc="sV")
                    sch.wait("sync", mdr)
                    sch.dma("sync", lambda e: e.dma_start(ap["dbg_rt"],
                                                          sb["dbg_rt_sb"][:]), "ddbg")
                m_route_free = m_cb
        m_comb_done = sch.mark("sV")
        if DEBUG:
            ap["dbg_sc0"] = nc.dram_tensor("dbg_sc0", [128, 16], F32,
                                           kind="ExternalOutput").ap()
            ap["dbg_rt"] = nc.dram_tensor("dbg_rt", [128, 112], F32,
                                          kind="ExternalOutput").ap()
            t("dbg_rt_sb", [128, 112], F32, es_p0)
            ap["dbg_comb"] = nc.dram_tensor("dbg_comb", [128, NTILE * 16], F32,
                                            kind="ExternalOutput").ap()
            ap["dbg_stats"] = nc.dram_tensor("dbg_stats", [128, 64], F32,
                                             kind="ExternalOutput").ap()
            sch.wait("sync", m_comb_done)
            sch.wait("sync", m_p0a_v)
            sch.dma("sync", lambda e: e.dma_start(ap["dbg_comb"], sb["comb"][:]), "ddbg")
            sch.dma("sync", lambda e: e.dma_start(ap["dbg_stats"], sb["statsA"][:]), "ddbg")
        m_p0b_dma = sch.mark("dxt")

        m_p0_done = [sch.mark("sP"), sch.mark("sV"), sch.mark("sA")]
        es_p0.close()
        t("bc_cache", [128, NST * 512], F32, es_moe)
        t("wcol", [128, 2 * 16 * 128], BF16, es_moe)
        t("wdT", [128, 8 * DM], BF16, es_moe)
        t("he", [128, 8 * SUB], BF16, es_moe)
        t("silu_t", [128, 512], F32, es_moe)

        if DEBUG:
            ap["dbg_wc"] = nc.dram_tensor("dbg_wc", [128, 2048], BF16,
                                          kind="ExternalOutput").ap()
            ap["dbg_he"] = nc.dram_tensor("dbg_he", [128, 8 * SUB], BF16,
                                          kind="ExternalOutput").ap()
            ap["dbg_r0"] = nc.dram_tensor("dbg_r0", [128, 4 * DM], F32,
                                          kind="ExternalOutput").ap()

        m_dtl_free = [None, None]
        m_bcq_free = [None, None, None]
        m_xr_free = None
        m_bank4_free = m_bank45_free[0]
        m_bank56_free = [m_bank45_free[1], None]
        m_scr2_free = [None, None]
        m_cache_free = [None] * NST
        m_cache_rdy = [None] * NST
        m_sT_free = None
        m_ybuf_free = None
        m_ytp = {}
        m_prep_v = {}
        m_dtt_rd = None
        m_dtx_rd = None
        m_state_upd = None
        m_scan_first = True
        m_scan_dma = None

        def emit_sT_block(cp):
            nonlocal m_bank4_free, m_sT_free
            yb = sb["ybuf"]
            for dti in range(8):
                sch.wait("tensor", m_ytp.get(("yadd", cp)))
                sch.wait("tensor", m_bank4_free)
                tp(yb[:, dti * LC:dti * LC + 128], ps[4][:, 0:128], inc=False)
                m_tp2 = tp(yb[:, dti * LC + 128:(dti + 1) * LC], ps[4][:, 128:256])
                sch.wait("vector", m_tp2)
                sch.wait("vector", m_sT_free)
                m_evs = sch.op("vector",
                               lambda e: e.tensor_copy(sb["sTst"][:, 0:256],
                                                       ps[4][:, 0:256]), inc="sV")
                m_bank4_free = m_evs
                sch.wait("scalar", m_evs)
                for hh in range(2):
                    m_sT_free = sch.dma(
                        "scalar",
                        lambda e, c_=cp, d=dti, h_=hh: e.dma_start(
                            ap["sT_all"][c_ * LC + h_ * 128:c_ * LC + (h_ + 1) * 128,
                                         d * 128:(d + 1) * 128],
                            sb["sTst"][:, h_ * 128:(h_ + 1) * 128]), "dsT")
            m_ytp[("sT", cp)] = sch.mark("sP")

        def emit_scan_pe(ci):
            nonlocal m_xr_free, m_bank4_free, m_scan_first
            nonlocal m_bank7_free
            par = ci % 2
            sch.wait("sync", m_dtl_free[par])
            sch.wait("sync", m_p0a_dma[0])
            sch.wait("sync", m_p0a_dma[1])
            m_dtl = sch.dma("sync",
                            lambda e, c_=ci, p=par: e.dma_start(
                                sb["dtl"][:, p * LC:(p + 1) * LC],
                                ap["dtl_dram"][:, c_ * LC:(c_ + 1) * LC]),
                            "dtl0" if par == 0 else "dtl1")
            def bc_load(q):
                slot = (ci * 4 + q) % 3
                ssem = f"dbc{slot}"
                sch.wait("sync", m_bcq_free[slot])
                sch.dma("sync",
                        lambda e, c_=ci, qq=q, sl=slot: e.dma_start(
                            sb["bcrow"][0:1, sl * 2048:sl * 2048 + 1024].rearrange(
                                "o (n t) -> o n t", n=4),
                            ap["bc_dram"][qq * 4:(qq + 1) * 4,
                                          c_ * LC:(c_ + 1) * LC].rearrange(
                                "(o n) t -> o n t", o=1)), ssem)
                return sch.dma("sync",
                               lambda e, c_=ci, qq=q, sl=slot: e.dma_start(
                                   sb["bcrow"][0:1, sl * 2048 + 1024:(sl + 1) * 2048].rearrange(
                                       "o (n t) -> o n t", n=4),
                                   ap["bc_dram"][16 + qq * 4:16 + (qq + 1) * 4,
                                                 c_ * LC:(c_ + 1) * LC].rearrange(
                                       "(o n) t -> o n t", o=1)), ssem)

            m_bc_l = [bc_load(0), bc_load(1), bc_load(2), None]
            for n in range(NST):
                q = n // 4
                slot = (ci * 4 + q) % 3
                bnk = 5 + (n % 2)
                sch.wait("tensor", m_ones)
                sch.wait("tensor", m_bc_l[q])
                sch.wait("tensor", m_bank56_free[n % 2])
                mm(sb["ones_sb"][0:1, :],
                   sb["bcrow"][0:1, slot * 2048 + (n % 4) * LC:
                               slot * 2048 + (n % 4 + 1) * LC],
                   ps[bnk][:, 0:256], start=True, stop=True, inc=False)
                m_bc = mm(sb["ones_sb"][0:1, :],
                          sb["bcrow"][0:1, slot * 2048 + 1024 + (n % 4) * LC:
                                      slot * 2048 + 1024 + (n % 4 + 1) * LC],
                          ps[bnk][:, 256:512], start=True, stop=True)
                if n % 4 == 3:
                    m_bcq_free[slot] = m_bc
                    if n == 3:
                        m_bc_l[3] = bc_load(3)
                sch.wait("scalar", m_bc)
                sch.wait("scalar", m_cache_free[n])
                m_ev = sch.op("scalar",
                              lambda e, nn=n, b_=bnk: e.activation(
                                  sb["bc_cache"][:, nn * 512:(nn + 1) * 512],
                                  ps[b_][:, 0:512], AF.Identity), inc="sA")
                m_cache_rdy[n] = m_ev
                m_bank56_free[n % 2] = m_ev
            for g in range(2):
                sch.wait("sync", m_xr_free)
                mx = sch.dma("sync",
                             lambda e, c_=ci, gg=g: e.dma_start(
                                 sb["xr"].ap().rearrange("p (tt f) -> p tt f", tt=2),
                                 ap["xsdh"][c_ * LC:(c_ + 1) * LC,
                                            gg * 512:(gg + 1) * 512].rearrange(
                                     "(tt p) f -> p tt f", p=128)), "dxr")
                m_xn_l = []
                for tt in range(2):
                    sch.wait("scalar", mx)
                    if m_scan_first:
                        sch.wait("scalar", m_p0a_v)
                    til = ci * 2 + tt
                    m_l = sch.op("scalar",
                                 lambda e, t_=tt, ti=til: e.activation(
                                     sb["xr"][:, t_ * 512:(t_ + 1) * 512],
                                     sb["xr"][:, t_ * 512:(t_ + 1) * 512], AF.Identity,
                                     scale=sb["statsA"][:, 2 * ti:2 * ti + 1],
                                     bias=sb["statsA"][:, 2 * ti + 1:2 * ti + 2]),
                                 inc="sA")
                    m_xn_l.append(m_l)
                m_scan_first = False
                for j in range(4):
                    dti = g * 4 + j
                    blk = dti
                    sch.wait("tensor", m_xn_l[0])
                    sch.wait("tensor", m_xn_l[1])
                    sch.wait("tensor", m_bank7_free)
                    tp(sb["xr"][:, j * 128:(j + 1) * 128], ps[7][:, 0:128], inc=False)
                    m_tp = tp(sb["xr"][:, 512 + j * 128:512 + (j + 1) * 128],
                              ps[7][:, 128:256])
                    sch.wait("vector", m_tp)
                    m_ev = sch.op("vector",
                                  lambda e, jj=j: e.tensor_copy(
                                      sb["xnsc"][:, jj * LC:(jj + 1) * LC], ps[7][:, 0:256]),
                                  inc="sV")
                    m_bank7_free = m_ev
                    sch.wait("tensor", m_dtl)
                    sch.wait("tensor", m_bank4_free)
                    m_dmm = mm(sb["wdtT_sb"][:, dti * 128:(dti + 1) * 128],
                               sb["dtl"][:, par * LC:(par + 1) * LC],
                               ps[4][:, 0:256], start=True, stop=True)
                    sch.wait("scalar", m_dmm)
                    sch.wait("scalar", m_scr2_free[j % 2])
                    m_e1 = sch.op("scalar",
                                  lambda e, jj=j, d=dti: e.activation(
                                      sb["scr2"][:, (jj % 2) * LC:(jj % 2 + 1) * LC],
                                      ps[4][:, 0:256], AF.Exp,
                                      bias=sb["dtb_sb"][:, d:d + 1]), inc="sA")
                    m_bank4_free = m_e1
                    sch.wait("scalar", m_dtt_rd)
                    m_dt = sch.op("scalar",
                                  lambda e, jj=j, b_=blk: e.activation(
                                      sb["dtt"][:, b_ * LC:(b_ + 1) * LC],
                                      sb["scr2"][:, (jj % 2) * LC:(jj % 2 + 1) * LC],
                                      AF.Ln, bias=1.0), inc="sA")
                    m_scr2_free[j % 2] = m_dt
                    sch.wait("vector", m_dt)
                    sch.wait("vector", m_dtx_rd)
                    sch.op("vector",
                           lambda e, jj=j, b_=blk: e.tensor_mul(
                               sb["dtx"][:, b_ * LC:(b_ + 1) * LC],
                               sb["dtt"][:, b_ * LC:(b_ + 1) * LC],
                               sb["xnsc"][:, jj * LC:(jj + 1) * LC]))
                    sch.wait("vector", m_ytp.get(("sT", ci - 1)))
                    sch.op("vector",
                           lambda e, jj=j, b_=blk, d=dti: e.tensor_scalar(
                               sb["ybuf"][:, b_ * LC:(b_ + 1) * LC],
                               sb["xnsc"][:, jj * LC:(jj + 1) * LC],
                               sb["dvec_sb"][:, d:d + 1], None, ALU.mult))
                m_xr_free = sch.mark("sP")
            m_prep_v[ci] = sch.mark("sV")
            m_dtl_free[par] = sch.mark("sP")

        def scan_dve_tasks(ci):
            tasks = []
            for g_ in range(2):
                for n_ in range(NST):
                    tasks.append((g_, n_))

            def emit_one(gn):
                nonlocal m_state_upd, m_dtt_rd, m_dtx_rd
                g, n = gn
                if True:
                    for j in range(4):
                        dti = g * 4 + j
                        blk = dti
                        jp = j % 2
                        sch.wait("scalar", m_scr2_free[jp])
                        m_dA = sch.op("scalar",
                                      lambda e, p_=jp, b_=blk, d=dti, nn=n: e.activation(
                                          sb["scr2"][:, p_ * LC:(p_ + 1) * LC],
                                          sb["dtt"][:, b_ * LC:(b_ + 1) * LC], AF.Exp,
                                          scale=sb["A_sb"][:, d * NST + nn:
                                                           d * NST + nn + 1]),
                                      inc="sA")
                        sch.wait("vector", m_cache_rdy[n])
                        sch.wait("vector", m_dA)
                        if ci == 0:
                            sch.wait("vector", m_state0)
                            sch.wait("vector", m_aneg)
                        sch.op("vector",
                               lambda e, jj=j, b_=blk, nn=n: e.tensor_mul(
                                   sb["ubuf"][:, jj * LC:(jj + 1) * LC],
                                   sb["dtx"][:, b_ * LC:(b_ + 1) * LC],
                                   sb["bc_cache"][:, nn * 512:nn * 512 + 256]))
                        m_sc2 = sch.op("vector",
                                       lambda e, jj=j, p_=jp, d=dti, nn=n:
                                       e.tensor_tensor_scan(
                                           sb["hbuf"][:, jj * LC:(jj + 1) * LC],
                                           sb["scr2"][:, p_ * LC:(p_ + 1) * LC],
                                           sb["ubuf"][:, jj * LC:(jj + 1) * LC],
                                           sb["state"][:, nn * NDT + d:nn * NDT + d + 1],
                                           ALU.mult, ALU.add), inc="sV")
                        m_scr2_free[jp] = m_sc2
                        sch.op("vector",
                               lambda e, jj=j, nn=n: e.tensor_mul(
                                   sb["ubuf"][:, jj * LC:(jj + 1) * LC],
                                   sb["hbuf"][:, jj * LC:(jj + 1) * LC],
                                   sb["bc_cache"][:, nn * 512 + 256:(nn + 1) * 512]))
                        m_ya = sch.op("vector",
                                      lambda e, jj=j, b_=blk: e.tensor_add(
                                          sb["ybuf"][:, b_ * LC:(b_ + 1) * LC],
                                          sb["ybuf"][:, b_ * LC:(b_ + 1) * LC],
                                          sb["ubuf"][:, jj * LC:(jj + 1) * LC]),
                                      inc="sV")
                    if g == 1:
                        m_cache_free[n] = m_ya
                    m_state_upd = sch.op("vector",
                                         lambda e, nn=n, gg=g: e.tensor_copy(
                                             sb["state"][:, nn * NDT + gg * 4:
                                                         nn * NDT + gg * 4 + 4],
                                             bass.AP(sb["hbuf"].ap().tensor,
                                                     sb["hbuf"].ap().offset + LC - 1,
                                                     [sb["hbuf"].ap().ap[0], [LC, 4]])),
                                         inc="sV")
                if gn == tasks[-1]:
                    m_ytp[("yadd", ci)] = m_state_upd
                    m_dtt_rd = sch.mark("sA")
                    m_dtx_rd = sch.mark("sV")

            return [(lambda gn=gn: emit_one(gn)) for gn in tasks]

        m_wcol_free = [None] * 2
        m_wdT_free = None
        m_xnT_free = None
        m_he_free = None
        m_ue_free = sch.mark("sA")
        m_silu_free = sch.mark("sA")
        m_silu_rd = [None, None]
        m_m2bank_free = [sch.mark("sA"), sch.mark("sA")]
        m_final_gate = []

        col_jobs = []
        for si_ in range(NSUB):
            for e2 in EORDER:
                for it_ in range(8):
                    for mi_ in range(2):
                        col_jobs.append((si_, e2, mi_, it_))
        col_loaded = {}
        nxt = {"i": 0}
        m_wcast_all = sch.mark("dW")

        def wcol_load(slot, e_i, m_i, it):
            src = ap["w16"][e_i, m_i].rearrange("(it p f) -> it p f", it=8, p=128)
            src = src[it]
            dst = sb["wcol"][:, slot * 2048:(slot + 1) * 2048]
            for mk in m_p0_done:
                sch.wait("scalar", mk)
            sch.wait("scalar", m_wcast_all)
            sch.wait("scalar", m_wcol_free[slot])
            return sch.dma("scalar", lambda e, d=dst, s=src: e.dma_start(d, s),
                           f"dwc{slot}")

        def ensure_col(idx):
            while nxt["i"] <= idx + 1 and nxt["i"] < len(col_jobs):
                _, e2, mi_, it_ = col_jobs[nxt["i"]]
                slot = nxt["i"] % 2
                col_loaded[nxt["i"]] = (slot, wcol_load(slot, e2, mi_, it_))
                nxt["i"] += 1

        job = {"i": 0}
        hold = {"m1last": None}
        m_xnT = None

        def emit_expert(si, e_i, scan_q=None):
            nonlocal m_wdT_free, m_he_free, m_ue_free, m_silu_free
            nonlocal m_m2bank_free

            def drain(k):
                while k > 0 and scan_q:
                    scan_q.pop(0)()
                    k -= 1
            for mk in m_p0_done:
                sch.wait("scalar", mk)
            sch.wait("scalar", m_wcast_all)
            sch.wait("scalar", m_wdT_free)
            wd_marks = []
            for hh in range(2):
                src = ap["w16"][e_i, 2].rearrange("(p f) -> p f", p=128)
                src = src[:, hh * 8192:(hh + 1) * 8192]
                dst = sb["wdT"][:, hh * 8192:(hh + 1) * 8192]
                wd_marks.append(sch.dma("scalar",
                                        lambda e, d=dst, s=src: e.dma_start(d, s), "dwd"))
            m_ge = {}
            m_hev = None
            for it in range(8):
                for m_i in range(2):
                    ensure_col(job["i"])
                    slot, m_c = col_loaded[job["i"]]
                    job["i"] += 1
                    bank = m_i
                    sch.wait("tensor", m_c)
                    sch.wait("tensor", m_xnT)
                    if m_i == 0:
                        sch.wait("tensor", m_silu_free)
                    else:
                        sch.wait("tensor", m_ue_free)
                    for kt in range(16):
                        m_mm = mm(sb["wcol"][:, slot * 2048 + kt * 128:
                                             slot * 2048 + (kt + 1) * 128],
                                  sb["xnT"][:, kt * 512:(kt + 1) * 512],
                                  ps[bank][:, 0:512],
                                  start=(kt == 0), stop=(kt == 15))
                    m_wcol_free[slot] = m_mm
                    m_ge[(m_i, it)] = m_mm
                sch.wait("scalar", m_ge[(0, it)])
                sch.wait("scalar", m_silu_rd[0])
                m_si = sch.op("scalar",
                              lambda e: e.activation(sb["silu_t"][:, 0:512],
                                                     ps[0][:, 0:512], AF.Silu), inc="sA")
                m_silu_free = m_si
                sch.wait("vector", m_si)
                sch.wait("vector", m_ge[(1, it)])
                sch.wait("vector", m_he_free)
                m_hev = sch.op("vector",
                               lambda e, i_=it: e.tensor_mul(
                                   sb["he"][:, i_ * 512:(i_ + 1) * 512],
                                   sb["silu_t"][:, 0:512], ps[1][:, 0:512]), inc="sV")
                m_silu_rd[0] = m_hev
                m_ue_free = m_hev
                drain(3)
            hold["m1last"] = m_ge[(1, 7)]
            m_mm2 = None
            m_rv = None
            for tt in range(4):
                tile_abs = si * 4 + tt
                for dc in range(4):
                    bank = 2 + (dc % 2)
                    sch.wait("tensor", m_hev)
                    sch.wait("tensor", wd_marks[1])
                    sch.wait("tensor", m_m2bank_free[dc % 2])
                    for kt in range(8):
                        m_mm2 = mm(sb["he"][:, kt * 512 + tt * 128:
                                            kt * 512 + (tt + 1) * 128],
                                   sb["wdT"][:, kt * 2048 + dc * 512:
                                             kt * 2048 + (dc + 1) * 512],
                                   ps[bank][:, 0:512],
                                   start=(kt == 0), stop=(kt == 7))
                    sch.wait("vector", m_mm2)
                    rsl = sb["routed"][:, tt * DM + dc * 512:tt * DM + (dc + 1) * 512]
                    if e_i == 16:
                        m_rv = sch.op("vector",
                                      lambda e, r=rsl, b_=bank: e.tensor_copy(
                                          r, ps[b_][:, 0:512]), inc="sV")
                    else:
                        sch.wait("vector", m_comb_done)
                        m_rv = sch.op("vector",
                                      lambda e, r=rsl, b_=bank, ta=tile_abs, ee=e_i:
                                      e.scalar_tensor_tensor(
                                          r, ps[b_][:, 0:512],
                                          sb["comb"][:, ta * 16 + ee:ta * 16 + ee + 1],
                                          r, ALU.mult, ALU.add), inc="sV")
                    m_m2bank_free[dc % 2] = m_rv
                drain(2)
            if DEBUG and si == 0 and e_i == 16:
                sch.wait("sync", m_hev)
                sch.dma("sync", lambda e: e.dma_start(ap["dbg_he"], sb["he"][:]), "ddbg")
                sch.wait("sync", m_rv)
                sch.dma("sync", lambda e: e.dma_start(ap["dbg_r0"], sb["routed"][:]), "ddbg")
            m_wdT_free = m_mm2
            m_he_free = m_mm2
            return m_rv

        sTf = ap["sT_all"].rearrange("r c -> (r c)")
        agof = ap["ag_out"].rearrange("r c -> (r c)")

        def dyn_copy(e, dst, flat_src, slot):
            rv = e.value_load(sb["sel_sb"][0:1, slot:slot + 1])
            src = flat_src[bass.ts(rv, TOK * HD)].rearrange("(r c) -> r c", r=TOK)
            return e.dma_start(dst, src)

        def emit_ag():
            sch.wait("gpsimd", m_sel)
            sch.wait("gpsimd", m_scan_dma)
            m_agin = sch.dma("gpsimd", lambda e: dyn_copy(e, ap["ag_in"], sTf, 1), "dag")
            sch.wait("gpsimd", m_agin)
            sch.op("gpsimd",
                   lambda e: e.collective_compute(
                       "AllGather", ALU.bypass,
                       replica_groups=[[0, 1], [2, 3], [4, 5], [6, 7]],
                       ins=[ap["ag_in"].opt()], outs=[ap["ag_out"].opt()]),
                   inc="cc")
            m_cc = sch.op("gpsimd",
                          lambda e: e.collective_compute(
                              "AllGather", ALU.bypass,
                              replica_groups=[[0, 1], [2, 3], [4, 5], [6, 7]],
                              ins=[ap["agb_in"].opt()], outs=[ap["agb_out"].opt()]),
                          inc="cc")
            m_smine = sch.dma("gpsimd",
                              lambda e: dyn_copy(e, ap["smine_dram"], sTf, 0), "dag")
            sch.wait("gpsimd", m_cc)
            return m_smine

        m_smine = None
        slots = [(si_, e2) for si_ in range(NSUB) for e2 in EORDER]
        for s_i, (si, e_i) in enumerate(slots):
            if e_i == EORDER[0]:
                sch.wait("scalar", m_p0b_dma)
                sch.wait("scalar", m_xnT_free)
                m_xnT = sch.dma("scalar",
                                lambda e, s_=si: e.dma_start(
                                    sb["xnT"][:],
                                    ap["xnT_dram"][:, s_ * 16 * 512:
                                                   (s_ + 1) * 16 * 512]), "dxn")
            ci = s_i // 2
            on_chunk = (s_i % 2 == 0) and ci < NCHUNK
            scan_q = []
            if on_chunk:
                if ci >= 1:
                    emit_sT_block(ci - 1)
                emit_scan_pe(ci)
                scan_q = scan_dve_tasks(ci)
            elif s_i == 2 * NCHUNK:
                emit_sT_block(NCHUNK - 1)
                m_scan_dma = sch.mark("dsT")
                m_smine = emit_ag()
            m_rv = emit_expert(si, e_i, scan_q)
            while scan_q:
                scan_q.pop(0)()
                if DEBUG and s_i == 0:
                    for nm2, wd2, src2 in (("dbg_bcc", NST * 512, "bc_cache"),
                                           ("dbg_dtt", 8 * LC, "dtt"),
                                           ("dbg_dtx", 8 * LC, "dtx"),
                                           ("dbg_yb", 8 * LC, "ybuf")):
                        ap[nm2] = nc.dram_tensor(nm2, [128, wd2], F32,
                                                 kind="ExternalOutput").ap()
                        sch.wait("sync", sch.mark("sV"))
                        sch.wait("sync", sch.mark("sA"))
                        sch.dma("sync", lambda e, d=ap[nm2], s=sb[src2]:
                                e.dma_start(d, s[:]), "ddbg")
            if e_i == EORDER[-1]:
                m_xnT_free = hold["m1last"]
                sch.wait("scalar", m_rv)
                m_rsp = sch.dma("scalar",
                                lambda e, s_=si: e.dma_start(
                                    ap["routed_dram"][:, s_ * 4 * DM:(s_ + 1) * 4 * DM],
                                    sb["routed"][:]), "drs")
                m_final_gate.append(m_rsp)
                m_m2bank_free = [m_rsp, m_rsp]

        sch.wait("gpsimd", m_final_gate[-1])
        m_soth = sch.dma("gpsimd", lambda e: dyn_copy(e, ap["sother_dram"], agof, 2), "dag")

        es_moe.close()
        es_scan.close()
        t("ffin", [128, 4 * DM], F32)
        t("rbuf", [128, 2 * DM], F32)
        sfin = sb["ffin"][:, 0 * DM:1 * DM]
        xfin = sb["ffin"][:, 1 * DM:2 * DM]
        gbuf = sb["ffin"][:, 2 * DM:3 * DM]
        tmpo = sb["ffin"][:, 3 * DM:4 * DM]
        m_ffree = None
        m_gbuf_free = None
        m_rb_free = [None, None]
        for jt in range(NTILE):
            rp = jt % 2
            sch.wait("sync", m_final_gate[-1])
            sch.wait("sync", m_smine)
            sch.wait("sync", m_soth)
            sch.wait("sync", m_ffree)
            m1 = sch.dma("sync", lambda e, j=jt: e.dma_start(
                sfin[:, 0:HD], ap["smine_dram"][j * 128:(j + 1) * 128, :]), "dfin")
            m2 = sch.dma("sync", lambda e, j=jt: e.dma_start(
                sfin[:, HD:DM], ap["sother_dram"][j * 128:(j + 1) * 128, :]), "dfin")
            m3 = sch.dma("sync", lambda e, j=jt: e.dma_start(
                xfin, ap["xmine"][j * 128:(j + 1) * 128, :]), "dfin")
            sch.wait("sync", m_rb_free[rp])
            m4 = sch.dma("sync", lambda e, j=jt, p=rp: e.dma_start(
                sb["rbuf"][:, p * DM:(p + 1) * DM],
                ap["routed_dram"][:, j * DM:(j + 1) * DM]), "dfin")
            sch.wait("scalar", m4)
            sch.wait("scalar", m_gbuf_free)
            m_g = sch.op("scalar",
                         lambda e, p=rp: e.activation(
                             gbuf, sb["rbuf"][:, p * DM:(p + 1) * DM], AF.Silu), inc="sA")
            m_rb_free[rp] = m_g
            sch.wait("vector", m_g)
            sch.wait("vector", m1)
            sch.wait("vector", m2)
            sch.wait("vector", m3)
            sch.op("vector", lambda e: e.tensor_mul(tmpo, sfin, gbuf))
            m_o = sch.op("vector", lambda e: e.tensor_add(gbuf, tmpo, xfin), inc="sV")
            m_ffree = m_o
            sch.wait("sync", m_o)
            m_out = sch.dma("sync", lambda e, j=jt: e.dma_start(
                ap["out"][j * 128:(j + 1) * 128, :], gbuf), "dout")
            m_gbuf_free = m_out
        sch.wait("sync", sch.mark("dout"))
        sch.wait("sync", sch.mark("dfin"))
        sch.wait("sync", sch.mark("ddbg"))
        sch.wait("gpsimd", sch.mark("dag"))
        sch.wait("scalar", sch.mark("drs"))

        block = es.enter_context(nc.Block())

        def emit(engname):
            def f(eng):
                for st in sch.steps[engname]:
                    if st[0] == "w":
                        eng.wait_ge(sems[st[1]], st[2])
                    else:
                        inst = st[1](eng)
                        if st[2] is not None:
                            inst.then_inc(sems[st[2]], st[3])
            return f

        block.tensor(emit("tensor"))
        block.vector(emit("vector"))
        block.scalar(emit("scalar"))
        block.gpsimd(emit("gpsimd"))
        block.sync(emit("sync"))

    return nc


_cache = {}


def kernel(**inputs):
    x = np.asarray(inputs["L0_attn_x"])
    xpw = np.asarray(inputs["L0_attn_x_proj_w"])
    dtw = np.asarray(inputs["L0_attn_dt_proj_w"])
    dtb = np.asarray(inputs["L0_attn_dt_proj_b"])
    alog = np.asarray(inputs["L0_attn_A_log"])
    dv = np.asarray(inputs["L0_attn_D"])
    gw = np.asarray(inputs["L0_ffn_gate_w"])
    eb = np.asarray(inputs["L0_ffn_expert_bias"])
    wg = np.asarray(inputs["L0_ffn_w_gate"])
    wu = np.asarray(inputs["L0_ffn_w_up"])
    wd = np.asarray(inputs["L0_ffn_w_down"])
    swg = np.asarray(inputs["L0_ffn_sw_gate"])
    swu = np.asarray(inputs["L0_ffn_sw_up"])
    swd = np.asarray(inputs["L0_ffn_sw_down"])

    if "nc" not in _cache:
        _cache["nc"] = build_program()
    nc = _cache["nc"]

    C = np.ascontiguousarray

    def roll_d(a, k, axis):
        return np.roll(a, -k * HD, axis=axis) if k else a

    wgT = np.concatenate([wg.transpose(0, 2, 1), swg.T[None]], 0)
    wuT = np.concatenate([wu.transpose(0, 2, 1), swu.T[None]], 0)
    wdT = np.concatenate([wd.transpose(0, 2, 1), swd.T[None]], 0)

    def arr_m1(w):
        return w.reshape(NEXP, 16, 128, 8, 128).transpose(0, 3, 2, 1, 4).reshape(
            NEXP, DM, II)

    def arr_m2(w):
        return w.reshape(NEXP, 8, 128, DM).transpose(0, 2, 1, 3).reshape(NEXP, II, DM)
    xpT = xpw.T
    gwT = gw.T
    identf = np.eye(128, dtype=np.float32)
    ebias = C(np.broadcast_to(eb[None, :], (128, 16)).astype(np.float32))

    per_k = {}
    for k in (0, 1):
        dh = slice(k * HD, (k + 1) * HD)
        per_k[k] = dict(
            wg=C(arr_m1(roll_d(wgT, k, 1))),
            wu=C(arr_m1(roll_d(wuT, k, 1))),
            wd=C(arr_m2(roll_d(wdT, k, 2))),
            xpT=C(roll_d(xpT, k, 0)),
            gw=C(roll_d(gwT, k, 0)),
            wdtT=C(dtw[dh].T),
            dtb=C(dtb[dh].reshape(NDT, 128).T),
            alog=C(alog[dh].reshape(NDT, 128, NST).transpose(1, 0, 2).reshape(128, NDT * NST)),
            dvec=C(dv[dh].reshape(NDT, 128).T),
            sel=np.array([[k, 1 - k, 1 - k, 0, 0, 0, 0, 0]], np.int32),
        )

    in_maps = []
    for core in range(8):
        b, k = core // 2, core % 2
        xb = x[b]
        d = dict(per_k[k])
        d["xscan"] = C(roll_d(xb, k, 1))
        d["xsdh"] = C(xb[:, k * HD:(k + 1) * HD])
        d["xmine"] = C(roll_d(xb[k * TOK:(k + 1) * TOK], k, 1))
        d["ebias"] = ebias
        d["identf"] = identf
        in_maps.append(d)

    res = run_bass_kernel_spmd(nc, in_maps, core_ids=list(range(8)))
    out = np.empty((NB, L, DM), np.float32)
    for core in range(8):
        b, k = core // 2, core % 2
        o = np.asarray(res.results[core]["out"])
        out[b, k * TOK:(k + 1) * TOK] = np.roll(o, k * HD, axis=1) if k else o
    return out


# version 11
# speedup vs baseline: 1.1364x; 1.1206x over previous
import contextlib
import numpy as np
import ml_dtypes
import concourse.bass as bass
import concourse.mybir as mybir
from concourse.bass_utils import run_bass_kernel_spmd

F32 = mybir.dt.float32
BF16 = mybir.dt.bfloat16
I32 = mybir.dt.int32
AF = mybir.ActivationFunctionType
ALU = mybir.AluOpType
AX = mybir.AxisListType

NB, L, DM, HD, NST, RK, NEXP, II = 4, 4096, 2048, 1024, 16, 128, 17, 1024
TOK = 2048
SUB = 512
NSUB = TOK // SUB
LC = 256
NCHUNK = L // LC
NDT = HD // 128
LN_EPS = 1e-5
NTILE = TOK // 128
DEBUG = False

ENGS = ("tensor", "vector", "scalar", "gpsimd", "sync")
EORDER = [16] + list(range(16))


class Sch:
    def __init__(self):
        self.steps = {e: [] for e in ENGS}
        self.ct = {}
        self.last_wait = {e: {} for e in ENGS}

    def op(self, eng, fn, inc=None, n=1):
        self.steps[eng].append(("o", fn, inc, n))
        if inc is not None:
            self.ct[inc] = self.ct.get(inc, 0) + n
            return (inc, self.ct[inc], eng)
        return None

    def dma(self, eng, fn, sem):
        return self.op(eng, fn, inc=sem, n=16)

    def mark(self, sem):
        return (sem, self.ct.get(sem, 0), None)

    def wait(self, eng, mark, force=False):
        if mark is None:
            return
        sem, val, peng = mark
        if val <= 0:
            return
        if peng == eng and not force:
            return
        lw = self.last_wait[eng]
        if lw.get(sem, -1) >= val:
            return
        lw[sem] = val
        self.steps[eng].append(("w", sem, val))


def build_program():
    nc = bass.Bass("TRN2", target_bir_lowering=False)
    ap = {}

    def din(name, shape, dt):
        ap[name] = nc.dram_tensor(name, shape, dt, kind="ExternalInput").ap()

    def dout(name, shape, dt):
        ap[name] = nc.dram_tensor(name, shape, dt, kind="ExternalOutput").ap()

    def dint(name, shape, dt, dbg=False):
        kind = "ExternalOutput" if (DEBUG and dbg) else "Internal"
        ap[name] = nc.dram_tensor(name, shape, dt, kind=kind).ap()

    din("xscan", [L, DM], F32)
    din("xsdh", [L, HD], F32)
    din("xmine", [TOK, DM], F32)
    din("xpT", [DM, 160], F32)
    din("wdtT", [RK, HD], F32)
    din("dtb", [128, NDT], F32)
    din("alog", [128, NDT * NST], F32)
    din("dvec", [128, NDT], F32)
    din("gw", [DM, 16], F32)
    din("ebias", [128, 16], F32)
    din("wg", [NEXP, DM, II], F32)
    din("wu", [NEXP, DM, II], F32)
    din("wd", [NEXP, II, DM], F32)
    din("identf", [128, 128], F32)
    din("sel", [1, 8], I32)
    dout("out", [TOK, DM], F32)

    dint("w16", [NEXP, 3, DM * II], BF16)
    dint("dtl_dram", [RK, L], BF16, dbg=True)
    dint("bc_dram", [32, L], F32, dbg=True)
    dint("xnT_dram", [128, NSUB * 16 * SUB], BF16, dbg=True)
    dint("sT_all", [L, HD], F32, dbg=True)
    dint("ag_in", [TOK, HD], F32)
    dint("ag_out", [2 * TOK, HD], F32)
    dint("agb_in", [2, 64], F32)
    dint("agb_out", [4, 64], F32)
    dint("smine_dram", [TOK, HD], F32, dbg=True)
    dint("sother_dram", [TOK, HD], F32, dbg=True)
    dint("routed_dram", [128, NTILE * DM], F32, dbg=True)

    sch = Sch()
    es = contextlib.ExitStack()
    es_scan = contextlib.ExitStack()
    es_p0 = contextlib.ExitStack()
    es_moe = contextlib.ExitStack()

    with es:
        sb = {}

        def t(name, shape, dt, stack=None):
            sb[name] = (stack or es).enter_context(nc.sbuf_tensor(name, shape, dt))
            return sb[name]

        t("xnT", [128, 16 * SUB], BF16)
        t("routed", [128, (SUB // 128) * DM], F32)
        t("comb", [128, NTILE * 16], F32)
        t("statsA", [128, 2 * (L // 128)], F32)
        t("A_sb", [128, NDT * NST], F32)
        t("wdtT_sb", [128, HD], BF16)
        t("ebias_sb", [128, 16], F32)
        t("dvec_sb", [128, NDT], F32)
        t("dtb_sb", [128, NDT], F32)
        t("identf_sb", [128, 128], F32)
        t("ones_sb", [1, 128], F32)
        t("sel_sb", [1, 8], I32)
        t("state", [128, NST * NDT], F32)
        t("stats1", [128, 12], F32)
        t("epscol", [128, 1], F32)
        t("negcol", [128, 1], F32)

        t("xr", [128, 2 * 512], F32, es_scan)
        t("xnsc", [128, 4 * LC], F32, es_scan)
        t("dtt", [128, 8 * LC], F32, es_scan)
        t("dtx", [128, 8 * LC], F32, es_scan)
        t("ybuf", [128, 8 * LC], F32, es_scan)
        t("hbuf", [128, 4 * LC], F32, es_scan)
        t("ubuf", [128, 4 * LC], F32, es_scan)
        t("scr2", [128, 2 * LC], F32, es_scan)
        t("dtl", [128, 2 * LC], BF16, es_scan)
        t("bcrow", [1, 3 * 2048], F32, es_scan)
        t("sTst", [128, LC], F32, es_scan)

        t("xpT_sb", [128, 16 * 160], BF16, es_p0)
        t("gw_sb", [128, 16 * 16], F32, es_p0)
        t("p0x", [128, 2 * DM], F32, es_p0)
        t("p0sq", [128, DM], F32, es_p0)
        t("p0xn", [128, 4 * DM], F32, es_p0)
        t("p0bt", [128, 512], BF16, es_p0)
        t("p0ft", [128, 512], F32, es_p0)
        t("p0dtl", [128, 2 * 512], BF16, es_p0)
        t("p0bc", [32, 2 * 512], F32, es_p0)
        t("p0sc", [128, 16], F32, es_p0)
        t("p0key", [128, 16], F32, es_p0)
        t("p0eq", [128, 16], F32, es_p0)
        t("p0acc", [128, 16], F32, es_p0)
        t("p0tmp", [128, 16], F32, es_p0)
        t("p0w", [128, 4], F32, es_p0)

        ps = {}
        for i in range(8):
            ps[i] = es.enter_context(nc.psum_tensor(f"pb{i}", [128, 512], F32))

        sems = {}
        for s_ in ("dSm", "dXa", "dXb", "dxr", "dtl0", "dtl1", "dbc0", "dbc1", "dbc2",
                   "dfin", "dout", "ddbg", "dSp0", "dSp1", "dxt", "dsT",
                   "dwc0", "dwc1", "dwc2", "dwc3", "dwd", "dxn", "drs",
                   "dW", "dG", "dag", "cc", "sP", "sV", "sA"):
            sems[s_] = es.enter_context(nc.semaphore(s_))

        def mm(lhsT, rhs, out, start, stop, inc=True):
            return sch.op("tensor",
                          lambda e, a=out, b=lhsT, c=rhs, s1=start, s2=stop:
                          e.matmul(a, b, c, start=s1, stop=s2),
                          inc="sP" if inc else None)

        def tp(in_ap, out_ap, inc=True):
            return sch.op("tensor",
                          lambda e, a=out_ap, b=in_ap: e.transpose(a, b, sb["identf_sb"][:]),
                          inc="sP" if inc else None)

        def sload(dst_ap, src_ap, sem="dSm", queue="sync"):
            return sch.dma(queue, lambda e, d=dst_ap, s=src_ap: e.dma_start(d, s), sem)

        m_xpT = sload(sb["xpT_sb"].ap().rearrange("p (kt m) -> p kt m", kt=16),
                      ap["xpT"].rearrange("(kt p) m -> p kt m", p=128),
                      sem="dG", queue="gpsimd")
        m_wdt = sload(sb["wdtT_sb"][:], ap["wdtT"], sem="dG", queue="gpsimd")
        m_gw = sload(sb["gw_sb"].ap().rearrange("p (dt e) -> p dt e", dt=16),
                     ap["gw"].rearrange("(dt p) e -> p dt e", p=128))
        m_eb = sload(sb["ebias_sb"][:], ap["ebias"])
        m_dv = sload(sb["dvec_sb"][:], ap["dvec"])
        m_db = sload(sb["dtb_sb"][:], ap["dtb"])
        m_alog = sload(sb["A_sb"][:], ap["alog"])
        m_idf = sload(sb["identf_sb"][:], ap["identf"])
        m_sel = sload(sb["sel_sb"][:], ap["sel"])

        m_total_sm = sch.mark("dSm")
        m_xpT = m_wdt = sch.mark("dG")
        m_gw = m_eb = m_dv = m_db = m_alog = m_idf = m_sel = m_total_sm
        m_ones = sch.op("vector", lambda e: e.memset(sb["ones_sb"][:], 1.0), inc="sV")
        m_eps = sch.op("vector", lambda e: e.memset(sb["epscol"][:], LN_EPS), inc="sV")
        m_neg = sch.op("vector", lambda e: e.memset(sb["negcol"][:], -1e30), inc="sV")
        m_state0 = sch.op("vector", lambda e: e.memset(sb["state"][:], 0.0), inc="sV")

        sch.wait("scalar", m_alog)
        m_aexp = sch.op("scalar",
                        lambda e: e.activation(sb["A_sb"][:], sb["A_sb"][:], AF.Exp),
                        inc="sA")
        sch.wait("vector", m_aexp)
        m_aneg = sch.op("vector",
                        lambda e: e.tensor_scalar_mul(sb["A_sb"][:], sb["A_sb"][:], -1.0),
                        inc="sV")

        wsrc = {0: ap["wg"], 1: ap["wu"], 2: ap["wd"]}
        wcast_mark = {}
        for e_i in EORDER:
            for m_i in range(3):
                csrc = wsrc[m_i][e_i].rearrange("a b -> (a b)").rearrange("(o f) -> o f", o=1)
                cdst = ap["w16"][e_i, m_i].rearrange("(o f) -> o f", o=1)
                wcast_mark[(e_i, m_i)] = sch.dma(
                    "gpsimd", lambda e, d=cdst, s=csrc: e.dma_start(d, s), "dW")

        state_h = {"prev_xn": None}

        def stats_and_xn(src_ap_t, row0, xn_out_ap, xt_slot, rstd, nmr, m_xt_free):
            c = sb["stats1"]
            sch.wait("sync", m_xt_free)
            xts = sb["p0x"][:, xt_slot * DM:(xt_slot + 1) * DM]
            mload = sch.dma("sync",
                            lambda e, d=xts, s=src_ap_t[row0:row0 + 128, :]:
                            e.dma_start(d, s), "dXa" if xt_slot == 0 else "dXb")
            sch.wait("vector", mload)
            sch.wait("vector", state_h["prev_xn"])

            def vop(fn):
                mk = sch.op("vector", fn, inc="sV")
                sch.wait("vector", mk, force=True)
                return mk

            vop(lambda e, x=xts: e.reduce_sum(c[:, 0:1], x, AX.X))
            vop(lambda e, x=xts: e.tensor_mul(sb["p0sq"][:], x, x))
            vop(lambda e: e.reduce_sum(c[:, 1:2], sb["p0sq"][:], AX.X))
            vop(lambda e: e.tensor_scalar_mul(c[:, 2:3], c[:, 0:1], 1.0 / DM))
            vop(lambda e: e.tensor_scalar_mul(c[:, 3:4], c[:, 1:2], 1.0 / DM))
            vop(lambda e: e.tensor_mul(c[:, 4:5], c[:, 2:3], c[:, 2:3]))
            m_var = vop(lambda e: e.tensor_sub(c[:, 5:6], c[:, 3:4], c[:, 4:5]))
            sch.wait("scalar", m_var)
            sch.wait("scalar", m_eps)
            m_ln = sch.op("scalar",
                          lambda e: e.activation(c[:, 6:7], c[:, 5:6], AF.Ln,
                                                 bias=sb["epscol"][:]), inc="sA")
            sch.wait("scalar", m_ln, force=True)
            m_r = sch.op("scalar",
                         lambda e, r=rstd: e.activation(r, c[:, 6:7], AF.Exp, scale=-0.5),
                         inc="sA")
            sch.wait("vector", m_r)
            m_nm = vop(lambda e, r=rstd, o=nmr:
                       e.scalar_tensor_tensor(o, c[:, 2:3], -1.0, r, ALU.mult, ALU.mult))
            sch.wait("scalar", m_nm)
            m_xn = sch.op("scalar",
                          lambda e, o=xn_out_ap, x=xts, r=rstd, n=nmr:
                          e.activation(o, x, AF.Identity, scale=r, bias=n),
                          inc="sA")
            state_h["prev_xn"] = m_xn
            return m_xn

        sch.wait("tensor", m_idf)
        sch.wait("tensor", m_xpT)
        sch.wait("tensor", m_gw)
        sch.wait("tensor", m_wdt)
        sch.wait("scalar", m_db)
        sch.wait("vector", m_dv)
        m_xt_free = [None, None]
        m_p0xn_free = [None] * 4
        m_bank7_free = None
        m_bank45_free = [None, None]
        m_p0stage_free = [None, None]
        m_p0bt_free = None

        for g in range(L // 512):
            m_xn_tiles = []
            for tt in range(4):
                ti = g * 4 + tt
                sch.wait("scalar", m_p0xn_free[tt])
                m_xn = stats_and_xn(ap["xscan"], g * 512 + tt * 128,
                                    sb["p0xn"][:, tt * DM:(tt + 1) * DM], tt % 2,
                                    sb["statsA"][:, 2 * ti:2 * ti + 1],
                                    sb["statsA"][:, 2 * ti + 1:2 * ti + 2],
                                    m_xt_free[tt % 2])
                m_xt_free[tt % 2] = sch.mark("sA")
                m_xn_tiles.append(m_xn)
            for dt_i in range(16):
                for tt in range(4):
                    sch.wait("tensor", m_xn_tiles[tt])
                sch.wait("tensor", m_bank7_free)
                for tt in range(4):
                    m_tp = tp(sb["p0xn"][:, tt * DM + dt_i * 128:tt * DM + (dt_i + 1) * 128],
                              ps[7][:, tt * 128:(tt + 1) * 128], inc=(tt == 3))
                sch.wait("vector", m_tp)
                sch.wait("vector", m_p0bt_free)
                m_ev = sch.op("vector",
                              lambda e: e.tensor_copy(sb["p0bt"][:], ps[7][:]),
                              inc="sV")
                m_bank7_free = m_ev
                sch.wait("tensor", m_ev)
                if dt_i == 0:
                    sch.wait("tensor", m_bank45_free[0])
                    sch.wait("tensor", m_bank45_free[1])
                mm(sb["xpT_sb"][:, dt_i * 160:dt_i * 160 + 128], sb["p0bt"][:],
                   ps[4][:, 0:512], start=(dt_i == 0), stop=(dt_i == 15))
                m_xdbl = mm(sb["xpT_sb"][:, dt_i * 160 + 128:dt_i * 160 + 160],
                            sb["p0bt"][:], ps[5][0:32, 0:512],
                            start=(dt_i == 0), stop=(dt_i == 15))
                m_p0bt_free = m_xdbl
            for tt in range(4):
                m_p0xn_free[tt] = sch.mark("sP")
            par = g % 2
            sch.wait("vector", m_xdbl)
            sch.wait("vector", m_p0stage_free[par])
            m_e1 = sch.op("vector",
                          lambda e, p=par: e.tensor_copy(
                              sb["p0dtl"][:, p * 512:(p + 1) * 512], ps[4][:]), inc="sV")
            m_e2 = sch.op("vector",
                          lambda e, p=par: e.tensor_copy(
                              sb["p0bc"][0:32, p * 512:(p + 1) * 512], ps[5][0:32, :]),
                          inc="sV")
            m_bank45_free = [m_e1, m_e2]
            sch.wait("scalar", m_e2)
            sch.dma("scalar",
                    lambda e, p=par, gg=g: e.dma_start(
                        ap["dtl_dram"][:, gg * 512:(gg + 1) * 512],
                        sb["p0dtl"][:, p * 512:(p + 1) * 512]), "dSp0" if par == 0 else "dSp1")
            m_d2 = sch.dma("scalar",
                           lambda e, p=par, gg=g: e.dma_start(
                               ap["bc_dram"][:, gg * 512:(gg + 1) * 512],
                               sb["p0bc"][0:32, p * 512:(p + 1) * 512]),
                           "dSp0" if par == 0 else "dSp1")
            m_p0stage_free[par] = m_d2
        m_p0a_dma = [sch.mark("dSp0"), sch.mark("dSp1")]
        m_p0a_v = sch.mark("sV")

        m_logit_free = [None] * 4
        m_stagef_free = None
        m_xnTst_free = m_p0bt_free
        m_route_free = None

        for g in range(NSUB):
            m_xn_tiles = []
            for tt in range(4):
                sch.wait("scalar", m_p0xn_free[tt])
                m_xn = stats_and_xn(ap["xmine"], g * 512 + tt * 128,
                                    sb["p0xn"][:, tt * DM:(tt + 1) * DM], tt % 2,
                                    sb["stats1"][:, 8:9], sb["stats1"][:, 9:10],
                                    m_xt_free[tt % 2])
                m_xt_free[tt % 2] = sch.mark("sA")
                m_xn_tiles.append(m_xn)
            for dt_i in range(16):
                for tt in range(4):
                    sch.wait("tensor", m_xn_tiles[tt])
                sch.wait("tensor", m_bank7_free)
                for tt in range(4):
                    m_tp = tp(sb["p0xn"][:, tt * DM + dt_i * 128:tt * DM + (dt_i + 1) * 128],
                              ps[7][:, tt * 128:(tt + 1) * 128], inc=(tt == 3))
                sch.wait("vector", m_tp)
                sch.wait("vector", m_xnTst_free)
                m_evb = sch.op("vector",
                               lambda e: e.tensor_copy(sb["p0bt"][:], ps[7][:]), inc="sV")
                sch.wait("vector", m_stagef_free)
                m_evf = sch.op("vector",
                               lambda e: e.tensor_copy(sb["p0ft"][:], ps[7][:]), inc="sV")
                m_bank7_free = m_evf
                sch.wait("scalar", m_evb)
                m_xnTst_free = sch.dma(
                    "scalar",
                    lambda e, gg=g, d=dt_i: e.dma_start(
                        ap["xnT_dram"][:, gg * 16 * 512 + d * 512:gg * 16 * 512 + (d + 1) * 512],
                        sb["p0bt"][:]), "dxt")
                sch.wait("tensor", m_evf)
                for tt in range(4):
                    if dt_i == 0:
                        sch.wait("tensor", m_logit_free[tt])
                    m_lg = mm(sb["p0ft"][:, tt * 128:(tt + 1) * 128],
                              sb["gw_sb"].ap().rearrange("p (dt e) -> p dt e", dt=16)[:, dt_i, :],
                              ps[tt][:, 0:16], start=(dt_i == 0), stop=(dt_i == 15))
                m_stagef_free = m_lg
            for tt in range(4):
                tile_abs = g * 4 + tt
                sch.wait("scalar", m_lg)
                sch.wait("scalar", m_route_free)
                sch.wait("vector", m_eb)
                m_sc0 = sch.op("scalar",
                               lambda e, b=tt: e.activation(sb["p0tmp"][:], ps[b][:, 0:16],
                                                            AF.Exp, scale=-1.0), inc="sA")
                sch.op("scalar",
                       lambda e: e.activation(sb["p0eq"][:], sb["p0tmp"][:], AF.Ln,
                                              bias=1.0))
                m_sc = sch.op("scalar",
                              lambda e: e.activation(sb["p0sc"][:], sb["p0eq"][:], AF.Exp,
                                                     scale=-1.0), inc="sA")
                m_logit_free[tt] = m_sc0
                if DEBUG and tile_abs == 0:
                    sch.wait("sync", m_sc)
                    sch.dma("sync", lambda e: e.dma_start(ap["dbg_sc0"], sb["p0sc"][:]), "ddbg")
                sch.wait("vector", m_sc)
                m_ka = sch.op("vector",
                              lambda e: e.tensor_add(sb["p0key"][:], sb["p0sc"][:],
                                                     sb["ebias_sb"][:]), inc="sV")
                sch.op("vector", lambda e: e.memset(sb["p0acc"][:], 0.0))
                c = sb["p0w"]
                for j in range(4):
                    sch.wait("vector", m_ka, force=True)
                    m_mj = sch.op("vector",
                                  lambda e: e.reduce_max(c[:, 0:1], sb["p0key"][:], AX.X),
                                  inc="sV")
                    sch.wait("vector", m_mj, force=True)
                    sch.op("vector",
                           lambda e: e.tensor_scalar(sb["p0eq"][:], sb["p0key"][:],
                                                     c[:, 0:1], None, ALU.is_equal))
                    sch.wait("vector", m_neg)
                    m_ka = sch.op("vector",
                                  lambda e: e.scalar_tensor_tensor(
                                      sb["p0key"][:], sb["p0eq"][:], sb["negcol"][:],
                                      sb["p0key"][:], ALU.mult, ALU.add), inc="sV")
                    m_pm = sch.op("vector",
                                  lambda e: e.tensor_mul(sb["p0tmp"][:], sb["p0eq"][:],
                                                         sb["p0sc"][:]), inc="sV")
                    sch.wait("vector", m_pm, force=True)
                    m_wj = sch.op("vector",
                                  lambda e: e.reduce_sum(c[:, 1:2], sb["p0tmp"][:], AX.X),
                                  inc="sV")
                    sch.wait("vector", m_wj, force=True)
                    if DEBUG and tile_abs == 0:
                        sch.op("vector", lambda e, jj=j: e.tensor_copy(
                            sb["dbg_rt_sb"][:, jj * 16:(jj + 1) * 16], sb["p0eq"][:]))
                        sch.op("vector", lambda e, jj=j: e.tensor_copy(
                            sb["dbg_rt_sb"][:, 64 + jj:65 + jj], c[:, 1:2]))
                    sch.op("vector",
                           lambda e: e.scalar_tensor_tensor(
                               sb["p0acc"][:], sb["p0eq"][:], c[:, 1:2], sb["p0acc"][:],
                               ALU.mult, ALU.add))
                    if j == 0:
                        m_wsum = sch.op("vector",
                                        lambda e: e.tensor_copy(c[:, 2:3], c[:, 1:2]),
                                        inc="sV")
                    else:
                        m_wsum = sch.op("vector",
                                        lambda e: e.tensor_add(c[:, 2:3], c[:, 2:3],
                                                               c[:, 1:2]), inc="sV")
                if DEBUG and tile_abs == 0:
                    sch.op("vector", lambda e: e.tensor_copy(
                        sb["dbg_rt_sb"][:, 68:69], c[:, 2:3]))
                    sch.op("vector", lambda e: e.tensor_copy(
                        sb["dbg_rt_sb"][:, 72:88], sb["p0acc"][:]))
                    sch.op("vector", lambda e: e.tensor_copy(
                        sb["dbg_rt_sb"][:, 96:112], sb["p0sc"][:]))
                sch.wait("vector", m_wsum, force=True)
                m_rn = sch.op("vector", lambda e: e.reciprocal(c[:, 3:4], c[:, 2:3]),
                              inc="sV")
                sch.wait("vector", m_rn, force=True)
                m_cb = sch.op("vector",
                              lambda e, ti=tile_abs: e.tensor_scalar(
                                  sb["comb"][:, ti * 16:(ti + 1) * 16], sb["p0acc"][:],
                                  c[:, 3:4], None, ALU.mult), inc="sV")
                if DEBUG and tile_abs == 0:
                    sch.op("vector", lambda e: e.tensor_copy(
                        sb["dbg_rt_sb"][:, 69:70], c[:, 3:4]))
                    mdr = sch.op("vector", lambda e: e.tensor_copy(
                        sb["dbg_rt_sb"][:, 70:71], c[:, 3:4]), inc="sV")
                    sch.wait("sync", mdr)
                    sch.dma("sync", lambda e: e.dma_start(ap["dbg_rt"],
                                                          sb["dbg_rt_sb"][:]), "ddbg")
                m_route_free = m_cb
        m_comb_done = sch.mark("sV")
        if DEBUG:
            ap["dbg_sc0"] = nc.dram_tensor("dbg_sc0", [128, 16], F32,
                                           kind="ExternalOutput").ap()
            ap["dbg_rt"] = nc.dram_tensor("dbg_rt", [128, 112], F32,
                                          kind="ExternalOutput").ap()
            t("dbg_rt_sb", [128, 112], F32, es_p0)
            ap["dbg_comb"] = nc.dram_tensor("dbg_comb", [128, NTILE * 16], F32,
                                            kind="ExternalOutput").ap()
            ap["dbg_stats"] = nc.dram_tensor("dbg_stats", [128, 64], F32,
                                             kind="ExternalOutput").ap()
            sch.wait("sync", m_comb_done)
            sch.wait("sync", m_p0a_v)
            sch.dma("sync", lambda e: e.dma_start(ap["dbg_comb"], sb["comb"][:]), "ddbg")
            sch.dma("sync", lambda e: e.dma_start(ap["dbg_stats"], sb["statsA"][:]), "ddbg")
        m_p0b_dma = sch.mark("dxt")

        m_p0_done = [sch.mark("sP"), sch.mark("sV"), sch.mark("sA")]
        es_p0.close()
        t("bc_cache", [128, NST * 512], F32, es_moe)
        t("wcol", [128, 3 * 16 * 128], BF16, es_moe)
        t("wdT", [128, 8 * DM], BF16, es_moe)
        t("he", [128, 8 * SUB], BF16, es_moe)
        t("silu_t", [128, 512], F32, es_moe)

        if DEBUG:
            ap["dbg_wc"] = nc.dram_tensor("dbg_wc", [128, 2048], BF16,
                                          kind="ExternalOutput").ap()
            ap["dbg_he"] = nc.dram_tensor("dbg_he", [128, 8 * SUB], BF16,
                                          kind="ExternalOutput").ap()
            ap["dbg_r0"] = nc.dram_tensor("dbg_r0", [128, 4 * DM], F32,
                                          kind="ExternalOutput").ap()

        m_dtl_free = [None, None]
        m_bcq_free = [None, None, None]
        m_xr_free = None
        m_bank4_free = m_bank45_free[0]
        m_bank56_free = [m_bank45_free[1], None]
        m_scr2_free = [None, None]
        m_cache_free = [None] * NST
        m_cache_rdy = [None] * NST
        m_sT_free = None
        m_ybuf_free = None
        m_ytp = {}
        m_prep_v = {}
        m_dtt_rd = None
        m_dtx_rd = None
        m_state_upd = None
        m_scan_first = True
        m_scan_dma = None

        def emit_sT_block(cp):
            nonlocal m_bank4_free, m_sT_free
            yb = sb["ybuf"]
            for dti in range(8):
                sch.wait("tensor", m_ytp.get(("yadd", cp)))
                sch.wait("tensor", m_bank4_free)
                tp(yb[:, dti * LC:dti * LC + 128], ps[4][:, 0:128], inc=False)
                m_tp2 = tp(yb[:, dti * LC + 128:(dti + 1) * LC], ps[4][:, 128:256])
                sch.wait("vector", m_tp2)
                sch.wait("vector", m_sT_free)
                m_evs = sch.op("vector",
                               lambda e: e.tensor_copy(sb["sTst"][:, 0:256],
                                                       ps[4][:, 0:256]), inc="sV")
                m_bank4_free = m_evs
                sch.wait("gpsimd", m_evs)
                for hh in range(2):
                    m_sT_free = sch.dma(
                        "gpsimd",
                        lambda e, c_=cp, d=dti, h_=hh: e.dma_start(
                            ap["sT_all"][c_ * LC + h_ * 128:c_ * LC + (h_ + 1) * 128,
                                         d * 128:(d + 1) * 128],
                            sb["sTst"][:, h_ * 128:(h_ + 1) * 128]), "dsT")
            m_ytp[("sT", cp)] = sch.mark("sP")

        def emit_scan_pe(ci):
            nonlocal m_xr_free, m_bank4_free, m_scan_first
            nonlocal m_bank7_free
            par = ci % 2
            sch.wait("sync", m_dtl_free[par])
            sch.wait("sync", m_p0a_dma[0])
            sch.wait("sync", m_p0a_dma[1])
            m_dtl = sch.dma("sync",
                            lambda e, c_=ci, p=par: e.dma_start(
                                sb["dtl"][:, p * LC:(p + 1) * LC],
                                ap["dtl_dram"][:, c_ * LC:(c_ + 1) * LC]),
                            "dtl0" if par == 0 else "dtl1")
            def bc_load(q):
                slot = (ci * 4 + q) % 3
                ssem = f"dbc{slot}"
                sch.wait("sync", m_bcq_free[slot])
                sch.dma("sync",
                        lambda e, c_=ci, qq=q, sl=slot: e.dma_start(
                            sb["bcrow"][0:1, sl * 2048:sl * 2048 + 1024].rearrange(
                                "o (n t) -> o n t", n=4),
                            ap["bc_dram"][qq * 4:(qq + 1) * 4,
                                          c_ * LC:(c_ + 1) * LC].rearrange(
                                "(o n) t -> o n t", o=1)), ssem)
                return sch.dma("sync",
                               lambda e, c_=ci, qq=q, sl=slot: e.dma_start(
                                   sb["bcrow"][0:1, sl * 2048 + 1024:(sl + 1) * 2048].rearrange(
                                       "o (n t) -> o n t", n=4),
                                   ap["bc_dram"][16 + qq * 4:16 + (qq + 1) * 4,
                                                 c_ * LC:(c_ + 1) * LC].rearrange(
                                       "(o n) t -> o n t", o=1)), ssem)

            m_bc_l = [bc_load(0), bc_load(1), bc_load(2), None]
            for n in range(NST):
                q = n // 4
                slot = (ci * 4 + q) % 3
                bnk = 5 + (n % 2)
                sch.wait("tensor", m_ones)
                sch.wait("tensor", m_bc_l[q])
                sch.wait("tensor", m_bank56_free[n % 2])
                mm(sb["ones_sb"][0:1, :],
                   sb["bcrow"][0:1, slot * 2048 + (n % 4) * LC:
                               slot * 2048 + (n % 4 + 1) * LC],
                   ps[bnk][:, 0:256], start=True, stop=True, inc=False)
                m_bc = mm(sb["ones_sb"][0:1, :],
                          sb["bcrow"][0:1, slot * 2048 + 1024 + (n % 4) * LC:
                                      slot * 2048 + 1024 + (n % 4 + 1) * LC],
                          ps[bnk][:, 256:512], start=True, stop=True)
                if n % 4 == 3:
                    m_bcq_free[slot] = m_bc
                    if n == 3:
                        m_bc_l[3] = bc_load(3)
                sch.wait("scalar", m_bc)
                sch.wait("scalar", m_cache_free[n])
                m_ev = sch.op("scalar",
                              lambda e, nn=n, b_=bnk: e.activation(
                                  sb["bc_cache"][:, nn * 512:(nn + 1) * 512],
                                  ps[b_][:, 0:512], AF.Identity), inc="sA")
                m_cache_rdy[n] = m_ev
                m_bank56_free[n % 2] = m_ev
            for g in range(2):
                sch.wait("sync", m_xr_free)
                mx = sch.dma("sync",
                             lambda e, c_=ci, gg=g: e.dma_start(
                                 sb["xr"].ap().rearrange("p (tt f) -> p tt f", tt=2),
                                 ap["xsdh"][c_ * LC:(c_ + 1) * LC,
                                            gg * 512:(gg + 1) * 512].rearrange(
                                     "(tt p) f -> p tt f", p=128)), "dxr")
                m_xn_l = []
                for tt in range(2):
                    sch.wait("scalar", mx)
                    if m_scan_first:
                        sch.wait("scalar", m_p0a_v)
                    til = ci * 2 + tt
                    m_l = sch.op("scalar",
                                 lambda e, t_=tt, ti=til: e.activation(
                                     sb["xr"][:, t_ * 512:(t_ + 1) * 512],
                                     sb["xr"][:, t_ * 512:(t_ + 1) * 512], AF.Identity,
                                     scale=sb["statsA"][:, 2 * ti:2 * ti + 1],
                                     bias=sb["statsA"][:, 2 * ti + 1:2 * ti + 2]),
                                 inc="sA")
                    m_xn_l.append(m_l)
                m_scan_first = False
                for j in range(4):
                    dti = g * 4 + j
                    blk = dti
                    sch.wait("tensor", m_xn_l[0])
                    sch.wait("tensor", m_xn_l[1])
                    sch.wait("tensor", m_bank7_free)
                    tp(sb["xr"][:, j * 128:(j + 1) * 128], ps[7][:, 0:128], inc=False)
                    m_tp = tp(sb["xr"][:, 512 + j * 128:512 + (j + 1) * 128],
                              ps[7][:, 128:256])
                    sch.wait("vector", m_tp)
                    m_ev = sch.op("vector",
                                  lambda e, jj=j: e.tensor_copy(
                                      sb["xnsc"][:, jj * LC:(jj + 1) * LC], ps[7][:, 0:256]),
                                  inc="sV")
                    m_bank7_free = m_ev
                    sch.wait("tensor", m_dtl)
                    sch.wait("tensor", m_bank4_free)
                    m_dmm = mm(sb["wdtT_sb"][:, dti * 128:(dti + 1) * 128],
                               sb["dtl"][:, par * LC:(par + 1) * LC],
                               ps[4][:, 0:256], start=True, stop=True)
                    sch.wait("scalar", m_dmm)
                    sch.wait("scalar", m_scr2_free[j % 2])
                    m_e1 = sch.op("scalar",
                                  lambda e, jj=j, d=dti: e.activation(
                                      sb["scr2"][:, (jj % 2) * LC:(jj % 2 + 1) * LC],
                                      ps[4][:, 0:256], AF.Exp,
                                      bias=sb["dtb_sb"][:, d:d + 1]), inc="sA")
                    m_bank4_free = m_e1
                    sch.wait("scalar", m_dtt_rd)
                    m_dt = sch.op("scalar",
                                  lambda e, jj=j, b_=blk: e.activation(
                                      sb["dtt"][:, b_ * LC:(b_ + 1) * LC],
                                      sb["scr2"][:, (jj % 2) * LC:(jj % 2 + 1) * LC],
                                      AF.Ln, bias=1.0), inc="sA")
                    m_scr2_free[j % 2] = m_dt
                    sch.wait("vector", m_dt)
                    sch.wait("vector", m_dtx_rd)
                    sch.op("vector",
                           lambda e, jj=j, b_=blk: e.tensor_mul(
                               sb["dtx"][:, b_ * LC:(b_ + 1) * LC],
                               sb["dtt"][:, b_ * LC:(b_ + 1) * LC],
                               sb["xnsc"][:, jj * LC:(jj + 1) * LC]))
                    sch.wait("vector", m_ytp.get(("sT", ci - 1)))
                    sch.op("vector",
                           lambda e, jj=j, b_=blk, d=dti: e.tensor_scalar(
                               sb["ybuf"][:, b_ * LC:(b_ + 1) * LC],
                               sb["xnsc"][:, jj * LC:(jj + 1) * LC],
                               sb["dvec_sb"][:, d:d + 1], None, ALU.mult))
                m_xr_free = sch.mark("sP")
            m_prep_v[ci] = sch.mark("sV")
            m_dtl_free[par] = sch.mark("sP")

        def scan_dve_tasks(ci):
            tasks = []
            for g_ in range(2):
                for n_ in range(NST):
                    tasks.append((g_, n_))

            def emit_one(gn):
                nonlocal m_state_upd, m_dtt_rd, m_dtx_rd
                g, n = gn
                if True:
                    for j in range(4):
                        dti = g * 4 + j
                        blk = dti
                        jp = j % 2
                        sch.wait("scalar", m_scr2_free[jp])
                        m_dA = sch.op("scalar",
                                      lambda e, p_=jp, b_=blk, d=dti, nn=n: e.activation(
                                          sb["scr2"][:, p_ * LC:(p_ + 1) * LC],
                                          sb["dtt"][:, b_ * LC:(b_ + 1) * LC], AF.Exp,
                                          scale=sb["A_sb"][:, d * NST + nn:
                                                           d * NST + nn + 1]),
                                      inc="sA")
                        sch.wait("vector", m_cache_rdy[n])
                        sch.wait("vector", m_dA)
                        if ci == 0:
                            sch.wait("vector", m_state0)
                            sch.wait("vector", m_aneg)
                        sch.op("vector",
                               lambda e, jj=j, b_=blk, nn=n: e.tensor_mul(
                                   sb["ubuf"][:, jj * LC:(jj + 1) * LC],
                                   sb["dtx"][:, b_ * LC:(b_ + 1) * LC],
                                   sb["bc_cache"][:, nn * 512:nn * 512 + 256]))
                        m_sc2 = sch.op("vector",
                                       lambda e, jj=j, p_=jp, d=dti, nn=n:
                                       e.tensor_tensor_scan(
                                           sb["hbuf"][:, jj * LC:(jj + 1) * LC],
                                           sb["scr2"][:, p_ * LC:(p_ + 1) * LC],
                                           sb["ubuf"][:, jj * LC:(jj + 1) * LC],
                                           sb["state"][:, nn * NDT + d:nn * NDT + d + 1],
                                           ALU.mult, ALU.add), inc="sV")
                        m_scr2_free[jp] = m_sc2
                        sch.op("vector",
                               lambda e, jj=j, nn=n: e.tensor_mul(
                                   sb["ubuf"][:, jj * LC:(jj + 1) * LC],
                                   sb["hbuf"][:, jj * LC:(jj + 1) * LC],
                                   sb["bc_cache"][:, nn * 512 + 256:(nn + 1) * 512]))
                        m_ya = sch.op("vector",
                                      lambda e, jj=j, b_=blk: e.tensor_add(
                                          sb["ybuf"][:, b_ * LC:(b_ + 1) * LC],
                                          sb["ybuf"][:, b_ * LC:(b_ + 1) * LC],
                                          sb["ubuf"][:, jj * LC:(jj + 1) * LC]),
                                      inc="sV")
                    if g == 1:
                        m_cache_free[n] = m_ya
                    m_state_upd = sch.op("vector",
                                         lambda e, nn=n, gg=g: e.tensor_copy(
                                             sb["state"][:, nn * NDT + gg * 4:
                                                         nn * NDT + gg * 4 + 4],
                                             bass.AP(sb["hbuf"].ap().tensor,
                                                     sb["hbuf"].ap().offset + LC - 1,
                                                     [sb["hbuf"].ap().ap[0], [LC, 4]])),
                                         inc="sV")
                if gn == tasks[-1]:
                    m_ytp[("yadd", ci)] = m_state_upd
                    m_dtt_rd = sch.mark("sA")
                    m_dtx_rd = sch.mark("sV")

            return [(lambda gn=gn: emit_one(gn)) for gn in tasks]

        m_wcol_free = [None] * 3
        m_wdT_free = None
        m_xnT_free = None
        m_he_free = None
        m_ue_free = sch.mark("sA")
        m_silu_free = sch.mark("sA")
        m_silu_rd = [None, None]
        m_m2bank_free = [sch.mark("sA"), sch.mark("sA")]
        m_final_gate = []

        col_jobs = []
        for si_ in range(NSUB):
            for e2 in EORDER:
                for it_ in range(8):
                    for mi_ in range(2):
                        col_jobs.append((si_, e2, mi_, it_))
        col_loaded = {}
        nxt = {"i": 0}
        m_wcast_all = sch.mark("dW")

        def wcol_load(slot, e_i, m_i, it):
            src = ap["w16"][e_i, m_i].rearrange("(it p f) -> it p f", it=8, p=128)
            src = src[it]
            dst = sb["wcol"][:, slot * 2048:(slot + 1) * 2048]
            for mk in m_p0_done:
                sch.wait("scalar", mk)
            sch.wait("scalar", m_wcast_all)
            sch.wait("scalar", m_wcol_free[slot])
            return sch.dma("scalar", lambda e, d=dst, s=src: e.dma_start(d, s),
                           f"dwc{slot}")

        def ensure_col(idx):
            while nxt["i"] <= idx + 2 and nxt["i"] < len(col_jobs):
                _, e2, mi_, it_ = col_jobs[nxt["i"]]
                slot = nxt["i"] % 3
                col_loaded[nxt["i"]] = (slot, wcol_load(slot, e2, mi_, it_))
                nxt["i"] += 1

        job = {"i": 0}
        hold = {"m1last": None}
        m_xnT = None

        def emit_expert(si, e_i, scan_q=None):
            nonlocal m_wdT_free, m_he_free, m_ue_free, m_silu_free
            nonlocal m_m2bank_free

            def drain(k):
                while k > 0 and scan_q:
                    scan_q.pop(0)()
                    k -= 1
            for mk in m_p0_done:
                sch.wait("gpsimd", mk)
            sch.wait("gpsimd", m_wcast_all)
            sch.wait("gpsimd", m_wdT_free)
            wd_marks = []
            for hh in range(2):
                src = ap["w16"][e_i, 2].rearrange("(p f) -> p f", p=128)
                src = src[:, hh * 8192:(hh + 1) * 8192]
                dst = sb["wdT"][:, hh * 8192:(hh + 1) * 8192]
                wd_marks.append(sch.dma("gpsimd",
                                        lambda e, d=dst, s=src: e.dma_start(d, s), "dwd"))
            m_ge = {}
            m_hev = None
            for it in range(8):
                for m_i in range(2):
                    ensure_col(job["i"])
                    slot, m_c = col_loaded[job["i"]]
                    job["i"] += 1
                    bank = m_i
                    sch.wait("tensor", m_c)
                    sch.wait("tensor", m_xnT)
                    if m_i == 0:
                        sch.wait("tensor", m_silu_free)
                    else:
                        sch.wait("tensor", m_ue_free)
                    for kt in range(16):
                        m_mm = mm(sb["wcol"][:, slot * 2048 + kt * 128:
                                             slot * 2048 + (kt + 1) * 128],
                                  sb["xnT"][:, kt * 512:(kt + 1) * 512],
                                  ps[bank][:, 0:512],
                                  start=(kt == 0), stop=(kt == 15))
                    m_wcol_free[slot] = m_mm
                    m_ge[(m_i, it)] = m_mm
                sch.wait("scalar", m_ge[(0, it)])
                sch.wait("scalar", m_silu_rd[0])
                m_si = sch.op("scalar",
                              lambda e: e.activation(sb["silu_t"][:, 0:512],
                                                     ps[0][:, 0:512], AF.Silu), inc="sA")
                m_silu_free = m_si
                sch.wait("vector", m_si)
                sch.wait("vector", m_ge[(1, it)])
                sch.wait("vector", m_he_free)
                m_hev = sch.op("vector",
                               lambda e, i_=it: e.tensor_mul(
                                   sb["he"][:, i_ * 512:(i_ + 1) * 512],
                                   sb["silu_t"][:, 0:512], ps[1][:, 0:512]), inc="sV")
                m_silu_rd[0] = m_hev
                m_ue_free = m_hev
                drain(2)
            hold["m1last"] = m_ge[(1, 7)]
            m_mm2 = None
            m_rv = None
            for tt in range(4):
                tile_abs = si * 4 + tt
                for dc in range(4):
                    bank = 2 + (dc % 2)
                    sch.wait("tensor", m_hev)
                    sch.wait("tensor", wd_marks[1])
                    sch.wait("tensor", m_m2bank_free[dc % 2])
                    for kt in range(8):
                        m_mm2 = mm(sb["he"][:, kt * 512 + tt * 128:
                                            kt * 512 + (tt + 1) * 128],
                                   sb["wdT"][:, kt * 2048 + dc * 512:
                                             kt * 2048 + (dc + 1) * 512],
                                   ps[bank][:, 0:512],
                                   start=(kt == 0), stop=(kt == 7))
                    sch.wait("vector", m_mm2)
                    rsl = sb["routed"][:, tt * DM + dc * 512:tt * DM + (dc + 1) * 512]
                    if e_i == 16:
                        m_rv = sch.op("vector",
                                      lambda e, r=rsl, b_=bank: e.tensor_copy(
                                          r, ps[b_][:, 0:512]), inc="sV")
                    else:
                        sch.wait("vector", m_comb_done)
                        m_rv = sch.op("vector",
                                      lambda e, r=rsl, b_=bank, ta=tile_abs, ee=e_i:
                                      e.scalar_tensor_tensor(
                                          r, ps[b_][:, 0:512],
                                          sb["comb"][:, ta * 16 + ee:ta * 16 + ee + 1],
                                          r, ALU.mult, ALU.add), inc="sV")
                    m_m2bank_free[dc % 2] = m_rv
                drain(2)
            if DEBUG and si == 0 and e_i == 16:
                sch.wait("sync", m_hev)
                sch.dma("sync", lambda e: e.dma_start(ap["dbg_he"], sb["he"][:]), "ddbg")
                sch.wait("sync", m_rv)
                sch.dma("sync", lambda e: e.dma_start(ap["dbg_r0"], sb["routed"][:]), "ddbg")
            m_wdT_free = m_mm2
            m_he_free = m_mm2
            return m_rv

        sTf = ap["sT_all"].rearrange("r c -> (r c)")
        agof = ap["ag_out"].rearrange("r c -> (r c)")

        def dyn_copy(e, dst, flat_src, slot):
            rv = e.value_load(sb["sel_sb"][0:1, slot:slot + 1])
            src = flat_src[bass.ts(rv, TOK * HD)].rearrange("(r c) -> r c", r=TOK)
            return e.dma_start(dst, src)

        def emit_ag():
            sch.wait("gpsimd", m_sel)
            sch.wait("gpsimd", m_scan_dma)
            m_agin = sch.dma("gpsimd", lambda e: dyn_copy(e, ap["ag_in"], sTf, 1), "dag")
            sch.wait("gpsimd", m_agin)
            sch.op("gpsimd",
                   lambda e: e.collective_compute(
                       "AllGather", ALU.bypass,
                       replica_groups=[[0, 1], [2, 3], [4, 5], [6, 7]],
                       ins=[ap["ag_in"].opt()], outs=[ap["ag_out"].opt()]),
                   inc="cc")
            m_cc = sch.op("gpsimd",
                          lambda e: e.collective_compute(
                              "AllGather", ALU.bypass,
                              replica_groups=[[0, 1], [2, 3], [4, 5], [6, 7]],
                              ins=[ap["agb_in"].opt()], outs=[ap["agb_out"].opt()]),
                          inc="cc")
            m_smine = sch.dma("gpsimd",
                              lambda e: dyn_copy(e, ap["smine_dram"], sTf, 0), "dag")
            sch.wait("gpsimd", m_cc)
            return m_smine

        m_smine = None
        scan_q = []
        slots = [(si_, e2) for si_ in range(NSUB) for e2 in EORDER]
        for s_i, (si, e_i) in enumerate(slots):
            if e_i == EORDER[0]:
                sch.wait("gpsimd", m_p0b_dma)
                sch.wait("gpsimd", m_xnT_free)
                m_xnT = sch.dma("gpsimd",
                                lambda e, s_=si: e.dma_start(
                                    sb["xnT"][:],
                                    ap["xnT_dram"][:, s_ * 16 * 512:
                                                   (s_ + 1) * 16 * 512]), "dxn")
            ci = s_i // 2
            on_chunk = (s_i % 2 == 0) and ci < NCHUNK
            if on_chunk:
                while scan_q:
                    scan_q.pop(0)()
                if ci >= 1:
                    emit_sT_block(ci - 1)
                emit_scan_pe(ci)
                scan_q.extend(scan_dve_tasks(ci))
            elif s_i == 2 * NCHUNK:
                while scan_q:
                    scan_q.pop(0)()
                emit_sT_block(NCHUNK - 1)
                m_scan_dma = sch.mark("dsT")
                m_smine = emit_ag()
            m_rv = emit_expert(si, e_i, scan_q)
            if e_i == EORDER[-1]:
                m_xnT_free = hold["m1last"]
                sch.wait("gpsimd", m_rv)
                m_rsp = sch.dma("gpsimd",
                                lambda e, s_=si: e.dma_start(
                                    ap["routed_dram"][:, s_ * 4 * DM:(s_ + 1) * 4 * DM],
                                    sb["routed"][:]), "drs")
                m_final_gate.append(m_rsp)
                m_m2bank_free = [m_rsp, m_rsp]

        sch.wait("gpsimd", m_final_gate[-1])
        m_soth = sch.dma("gpsimd", lambda e: dyn_copy(e, ap["sother_dram"], agof, 2), "dag")

        es_moe.close()
        es_scan.close()
        t("ffin", [128, 4 * DM], F32)
        t("rbuf", [128, 2 * DM], F32)
        sfin = sb["ffin"][:, 0 * DM:1 * DM]
        xfin = sb["ffin"][:, 1 * DM:2 * DM]
        gbuf = sb["ffin"][:, 2 * DM:3 * DM]
        tmpo = sb["ffin"][:, 3 * DM:4 * DM]
        m_ffree = None
        m_gbuf_free = None
        m_rb_free = [None, None]
        for jt in range(NTILE):
            rp = jt % 2
            sch.wait("sync", m_final_gate[-1])
            sch.wait("sync", m_smine)
            sch.wait("sync", m_soth)
            sch.wait("sync", m_ffree)
            m1 = sch.dma("sync", lambda e, j=jt: e.dma_start(
                sfin[:, 0:HD], ap["smine_dram"][j * 128:(j + 1) * 128, :]), "dfin")
            m2 = sch.dma("sync", lambda e, j=jt: e.dma_start(
                sfin[:, HD:DM], ap["sother_dram"][j * 128:(j + 1) * 128, :]), "dfin")
            m3 = sch.dma("sync", lambda e, j=jt: e.dma_start(
                xfin, ap["xmine"][j * 128:(j + 1) * 128, :]), "dfin")
            sch.wait("sync", m_rb_free[rp])
            m4 = sch.dma("sync", lambda e, j=jt, p=rp: e.dma_start(
                sb["rbuf"][:, p * DM:(p + 1) * DM],
                ap["routed_dram"][:, j * DM:(j + 1) * DM]), "dfin")
            sch.wait("scalar", m4)
            sch.wait("scalar", m_gbuf_free)
            m_g = sch.op("scalar",
                         lambda e, p=rp: e.activation(
                             gbuf, sb["rbuf"][:, p * DM:(p + 1) * DM], AF.Silu), inc="sA")
            m_rb_free[rp] = m_g
            sch.wait("vector", m_g)
            sch.wait("vector", m1)
            sch.wait("vector", m2)
            sch.wait("vector", m3)
            sch.op("vector", lambda e: e.tensor_mul(tmpo, sfin, gbuf))
            m_o = sch.op("vector", lambda e: e.tensor_add(gbuf, tmpo, xfin), inc="sV")
            m_ffree = m_o
            sch.wait("sync", m_o)
            m_out = sch.dma("sync", lambda e, j=jt: e.dma_start(
                ap["out"][j * 128:(j + 1) * 128, :], gbuf), "dout")
            m_gbuf_free = m_out
        sch.wait("sync", sch.mark("dout"))
        sch.wait("sync", sch.mark("dfin"))
        sch.wait("sync", sch.mark("ddbg"))
        sch.wait("gpsimd", sch.mark("dag"))
        sch.wait("gpsimd", sch.mark("drs"))

        block = es.enter_context(nc.Block())

        def emit(engname):
            def f(eng):
                for st in sch.steps[engname]:
                    if st[0] == "w":
                        eng.wait_ge(sems[st[1]], st[2])
                    else:
                        inst = st[1](eng)
                        if st[2] is not None:
                            inst.then_inc(sems[st[2]], st[3])
            return f

        block.tensor(emit("tensor"))
        block.vector(emit("vector"))
        block.scalar(emit("scalar"))
        block.gpsimd(emit("gpsimd"))
        block.sync(emit("sync"))

    return nc


_cache = {}


def kernel(**inputs):
    x = np.asarray(inputs["L0_attn_x"])
    xpw = np.asarray(inputs["L0_attn_x_proj_w"])
    dtw = np.asarray(inputs["L0_attn_dt_proj_w"])
    dtb = np.asarray(inputs["L0_attn_dt_proj_b"])
    alog = np.asarray(inputs["L0_attn_A_log"])
    dv = np.asarray(inputs["L0_attn_D"])
    gw = np.asarray(inputs["L0_ffn_gate_w"])
    eb = np.asarray(inputs["L0_ffn_expert_bias"])
    wg = np.asarray(inputs["L0_ffn_w_gate"])
    wu = np.asarray(inputs["L0_ffn_w_up"])
    wd = np.asarray(inputs["L0_ffn_w_down"])
    swg = np.asarray(inputs["L0_ffn_sw_gate"])
    swu = np.asarray(inputs["L0_ffn_sw_up"])
    swd = np.asarray(inputs["L0_ffn_sw_down"])

    if "nc" not in _cache:
        _cache["nc"] = build_program()
    nc = _cache["nc"]

    C = np.ascontiguousarray

    def roll_d(a, k, axis):
        return np.roll(a, -k * HD, axis=axis) if k else a

    wgT = np.concatenate([wg.transpose(0, 2, 1), swg.T[None]], 0)
    wuT = np.concatenate([wu.transpose(0, 2, 1), swu.T[None]], 0)
    wdT = np.concatenate([wd.transpose(0, 2, 1), swd.T[None]], 0)

    def arr_m1(w):
        return w.reshape(NEXP, 16, 128, 8, 128).transpose(0, 3, 2, 1, 4).reshape(
            NEXP, DM, II)

    def arr_m2(w):
        return w.reshape(NEXP, 8, 128, DM).transpose(0, 2, 1, 3).reshape(NEXP, II, DM)
    xpT = xpw.T
    gwT = gw.T
    identf = np.eye(128, dtype=np.float32)
    ebias = C(np.broadcast_to(eb[None, :], (128, 16)).astype(np.float32))

    per_k = {}
    for k in (0, 1):
        dh = slice(k * HD, (k + 1) * HD)
        per_k[k] = dict(
            wg=C(arr_m1(roll_d(wgT, k, 1))),
            wu=C(arr_m1(roll_d(wuT, k, 1))),
            wd=C(arr_m2(roll_d(wdT, k, 2))),
            xpT=C(roll_d(xpT, k, 0)),
            gw=C(roll_d(gwT, k, 0)),
            wdtT=C(dtw[dh].T),
            dtb=C(dtb[dh].reshape(NDT, 128).T),
            alog=C(alog[dh].reshape(NDT, 128, NST).transpose(1, 0, 2).reshape(128, NDT * NST)),
            dvec=C(dv[dh].reshape(NDT, 128).T),
            sel=np.array([[k, 1 - k, 1 - k, 0, 0, 0, 0, 0]], np.int32),
        )

    in_maps = []
    for core in range(8):
        b, k = core // 2, core % 2
        xb = x[b]
        d = dict(per_k[k])
        d["xscan"] = C(roll_d(xb, k, 1))
        d["xsdh"] = C(xb[:, k * HD:(k + 1) * HD])
        d["xmine"] = C(roll_d(xb[k * TOK:(k + 1) * TOK], k, 1))
        d["ebias"] = ebias
        d["identf"] = identf
        in_maps.append(d)

    res = run_bass_kernel_spmd(nc, in_maps, core_ids=list(range(8)))
    out = np.empty((NB, L, DM), np.float32)
    for core in range(8):
        b, k = core // 2, core % 2
        o = np.asarray(res.results[core]["out"])
        out[b, k * TOK:(k + 1) * TOK] = np.roll(o, k * HD, axis=1) if k else o
    return out
